# version 01
# baseline (speedup 1.0000x reference)
import numpy as np
import ml_dtypes
from contextlib import ExitStack
from collections import defaultdict

import concourse.bass as bass
import concourse.mybir as mybir
from concourse.bass_utils import run_bass_kernel_spmd

F32 = mybir.dt.float32
BF16 = mybir.dt.bfloat16
AF = mybir.ActivationFunctionType
ALU = mybir.AluOpType
AX = mybir.AxisListType
BF = ml_dtypes.bfloat16

NCORES = 8
B_CORE = 4
SEQ = 2048
TOK = B_CORE * SEQ
D = 1024
DI = 2048
TT = 512
NTT = TOK // TT
TPS = SEQ // TT
NEG = np.float32(-1e30)
FFN_I = 2730
FFN_IP = 2816
WIN = 256
EPS = np.float32(1e-5)


def _t(a):
    return np.ascontiguousarray(np.asarray(a).transpose())

def _bias128(b):
    return np.ascontiguousarray(np.asarray(b).reshape(-1, 128).T)

def _rotary_tables():
    n, DH, W = SEQ, 64, WIN
    half = np.arange(0, DH, 2, dtype=np.float32)
    inv_freq = 1.0 / 10000.0 ** (half / DH)
    t = np.arange(n, dtype=np.float32)
    freqs = t[:, None] * inv_freq[None, :]
    freqs = np.concatenate([freqs, freqs], -1)
    base = (half + 0.4 * DH) / (1.4 * DH)
    power = (t - n // 2) / (W // 2)
    scale = base[None, :] ** power[:, None]
    scale = np.concatenate([scale, scale], -1).astype(np.float32)
    def fm2(x):
        x = np.ascontiguousarray(x.T.astype(np.float32))
        return np.ascontiguousarray(np.concatenate([x, x], 0)).astype(BF)
    return (fm2(np.cos(freqs) * scale * 8.0), fm2(np.sin(freqs) * scale * 8.0),
            fm2(np.cos(freqs) / scale), fm2(np.sin(freqs) / scale))

def _band_masks():
    dm = np.arange(128)[:, None]
    f = np.arange(384)[None, :]
    band = (f >= dm) & (f <= dm + 256)
    m = np.zeros((3, 128, 384), np.float32)
    m[0] = np.where(band, 0.0, NEG)
    m[1] = np.where(band & (f >= 128), 0.0, NEG)
    m[2] = np.where(band & (f >= 256), 0.0, NEG)
    return np.ascontiguousarray(m.astype(BF))

def prep_inputs(p):
    sh = {}
    sh["w_in"] = np.ascontiguousarray(p["L0_attn_w_in"], dtype=np.float32)
    sh["b_in128"] = _bias128(p["L0_attn_b_in"])
    sh["convw128"] = np.ascontiguousarray(
        np.asarray(p["L0_attn_conv_w"])[:, 0, :].reshape(16, 128, 4)
        .transpose(1, 0, 2).reshape(128, 64)).astype(np.float32)
    sh["convb128"] = _bias128(p["L0_attn_conv_b"])
    sh["w_dt"] = np.ascontiguousarray(p["L0_attn_w_dt"], dtype=np.float32)
    sh["b_dt128"] = _bias128(p["L0_attn_b_dt"])
    sh["dskip128"] = _bias128(p["L0_attn_D_skip"])
    sh["w_out"] = np.ascontiguousarray(p["L0_attn_w_out"], dtype=np.float32)
    sh["b_out128"] = _bias128(p["L0_attn_b_out"])
    sh["lng128"] = _bias128(p["L0_attn_ln_g"])
    sh["lnb128"] = _bias128(p["L0_attn_ln_b"])
    sh["gate_wT"] = _t(p["L0_ffn_gate_w"])
    sh["gate_b8"] = np.ascontiguousarray(np.asarray(p["L0_ffn_gate_bias"],
                                                    dtype=np.float32).reshape(1, 8))
    sh["w1T"] = np.ascontiguousarray(np.asarray(p["L0_ffn_w1"]).transpose(0, 2, 1))
    sh["w3T"] = np.ascontiguousarray(np.asarray(p["L0_ffn_w3"]).transpose(0, 2, 1))
    sh["w2T"] = np.ascontiguousarray(np.asarray(p["L0_ffn_w2"]).transpose(0, 2, 1))
    sh["sw1T"] = _t(p["L0_ffn_sw1"])
    sh["sw3T"] = _t(p["L0_ffn_sw3"])
    sh["sw2T"] = _t(p["L0_ffn_sw2"])
    sh["WqkvT"] = _t(p["L1_attn_W_qkv"])
    sh["WprojT"] = _t(p["L1_attn_W_proj"])
    sh["bproj128"] = _bias128(p["L1_attn_b_proj"])
    sh["Wqkv6"] = np.ascontiguousarray(p["L1_ffn_Wqkv"], dtype=np.float32)
    sh["Wo6"] = np.ascontiguousarray(p["L1_ffn_Wo"], dtype=np.float32)
    W1 = np.asarray(p["L1_ffn_W1"], dtype=np.float32)
    W1p = np.zeros((6, 1024, 2 * FFN_IP), np.float32)
    W1p[:, :, :FFN_I] = W1[:, :, :FFN_I]
    W1p[:, :, FFN_IP:FFN_IP + FFN_I] = W1[:, :, FFN_I:]
    sh["W1p"] = W1p
    b1 = np.asarray(p["L1_ffn_b1"], dtype=np.float32)
    b1p = np.zeros((6, 2 * FFN_IP), np.float32)
    b1p[:, :FFN_I] = b1[:, :FFN_I]
    b1p[:, FFN_IP:FFN_IP + FFN_I] = b1[:, FFN_I:]
    sh["b1p128"] = np.stack([_bias128(b1p[i]) for i in range(6)])
    W2 = np.asarray(p["L1_ffn_W2"], dtype=np.float32)
    W2p = np.zeros((6, FFN_IP, 1024), np.float32)
    W2p[:, :FFN_I] = W2
    sh["W2p"] = W2p
    sh["b2128"] = np.stack([_bias128(np.asarray(p["L1_ffn_b2"])[i]) for i in range(6)])
    sh["flng128"] = np.stack([_bias128(np.asarray(p["L1_ffn_ln_g"])[i]) for i in range(6)])
    sh["flnb128"] = np.stack([_bias128(np.asarray(p["L1_ffn_ln_b"])[i]) for i in range(6)])
    qs = np.asarray(p["L1_ffn_q_scale"], dtype=np.float32)
    ks = np.asarray(p["L1_ffn_k_scale"], dtype=np.float32)
    sh["qs6"] = np.ascontiguousarray(np.tile(qs, (1, 2)).reshape(6, 128, 1))
    sh["ks6"] = np.ascontiguousarray(np.tile(ks, (1, 2)).reshape(6, 128, 1))
    sh["ident"] = np.eye(128, dtype=np.float32).astype(BF)
    sh["identf"] = np.eye(128, dtype=np.float32)
    ones2 = np.zeros((128, 2), np.float32)
    ones2[:64, 0] = 1.0
    ones2[64:, 1] = 1.0
    sh["ones2"] = ones2.astype(BF)
    sel2 = np.zeros((2, 128), np.float32)
    sel2[0, :64] = 1.0
    sel2[1, 64:] = 1.0
    sh["sel2f"] = sel2
    sh["masks"] = _band_masks()
    sh["csq"], sh["ssq"], sh["csk"], sh["ssk"] = _rotary_tables()
    x = np.asarray(p["L0_attn_x"], dtype=np.float32)
    percore = []
    for c in range(NCORES):
        xs = x[c * B_CORE:(c + 1) * B_CORE].reshape(TOK, D)
        percore.append({"x_fm": np.ascontiguousarray(xs.T)})
    return sh, percore


class Ctx:
    def __init__(self, nc):
        self.nc = nc
        self.stack = ExitStack()
        self.frames = [self.stack]
        self.sems = {}
        self.cnt = defaultdict(int)
        self.phase = 0
        self.gidc = {}
        self.R = {}

    def push_frame(self):
        self.frames.append(ExitStack())

    def pop_frame(self):
        assert len(self.frames) > 1, "attempt to pop the root frame"
        self.frames.pop().close()

    def sem(self, name):
        if name not in self.sems:
            self.sems[name] = self.stack.enter_context(self.nc.semaphore(name))
        return self.sems[name]

    def inc(self, inst, name, amt=1):
        inst.then_inc(self.sem(name), amt)
        self.cnt[name] += amt
        return self.cnt[name]

    def wait(self, eng, name, val=None):
        v = self.cnt[name] if val is None else val
        if v > 0:
            eng.wait_ge(self.sem(name), v)

    def sbuf(self, name, shape, dtype):
        return self.frames[-1].enter_context(self.nc.sbuf_tensor(name, shape, dtype))

    def psum(self, name, shape, dtype=F32):
        return self.frames[-1].enter_context(self.nc.psum_tensor(name, shape, dtype))

    def barrier(self):
        nc = self.nc
        self.phase += 1
        for s in list(self.cnt.keys()):
            if s != "phase":
                self.wait(nc.gpsimd, s)
        nc.gpsimd.sem_inc(self.sem("phase"), 1)
        self.cnt["phase"] = self.phase
        for eng in (nc.sync, nc.tensor, nc.vector, nc.scalar):
            self.wait(eng, "phase")

    def close(self):
        self.stack.close()


NROT = 8

class DmaGroup:
    def __init__(self, ctx, base):
        ctx.gidc[base] = ctx.gidc.get(base, 0) + 1
        self.ctx = ctx
        self.name = f"{base}{ctx.gidc[base] % NROT}"
        self.target = ctx.cnt[self.name]

    def ld(self, dst, src, **kw):
        self.target = self.ctx.inc(self.ctx.nc.sync.dma_start(dst, src, **kw), self.name, 16)
        return self.target

    def ld2(self, dst, src, **kw):
        self.target = self.ctx.inc(self.ctx.nc.scalar.dma_start(dst, src, **kw), self.name, 16)
        return self.target

    def st(self, dst, src, **kw):
        self.target = self.ctx.inc(self.ctx.nc.gpsimd.dma_start(dst, src, **kw), self.name, 16)
        return self.target

    def wait(self, eng):
        eng.wait_ge(self.ctx.sem(self.name), self.target)


SCRATCH = {
    "xn0f": ([D, TOK], F32),
    "xn0b": ([D, TOK], BF16),
    "xs_s": ([DI, B_CORE * (SEQ + 3)], F32),
    "sres": ([DI, TOK], F32),
    "xs2f": ([DI, B_CORE * (SEQ + 3)], F32),
    "xs2b": ([DI, TOK], BF16),
    "sa": ([DI, TOK], F32),
    "sb": ([DI, TOK], F32),
    "yg": ([DI, TOK], BF16),
    "p1": ([D, TOK], F32),
    "gfm": ([8, TOK], F32),
    "x1": ([D, TOK], F32),
    "uA": ([D, B_CORE * (SEQ + 1)], F32),
    "uB": ([D, B_CORE * (SEQ + 1)], F32),
    "xn1b": ([D, TOK], BF16),
    "q1s": ([D, TOK], BF16),
    "k1s": ([D, TOK], BF16),
    "v1s": ([D, TOK], BF16),
    "ao": ([D, TOK], BF16),
    "p2": ([D, TOK], F32),
    "qscr": ([D, TOK], BF16),
    "kscr": ([D, B_CORE * (WIN + SEQ)], BF16),
    "vscr": ([B_CORE * 16 * 18 * 128, 64], BF16),
    "xnf": ([D, TOK], BF16),
    "hgt": ([FFN_IP, TOK], BF16),
}

INPUTS = {
    "x_fm": ([D, TOK], F32),
    "w_in": ([D, 4096], F32), "b_in128": ([128, 32], F32),
    "convw128": ([128, 64], F32), "convb128": ([128, 16], F32),
    "w_dt": ([DI, DI], F32), "b_dt128": ([128, 16], F32),
    "dskip128": ([128, 16], F32),
    "w_out": ([DI, D], F32), "b_out128": ([128, 8], F32),
    "lng128": ([128, 8], F32), "lnb128": ([128, 8], F32),
    "gate_wT": ([D, 8], F32), "gate_b8": ([1, 8], F32),
    "w1T": ([8, D, 1024], F32), "w3T": ([8, D, 1024], F32), "w2T": ([8, 1024, D], F32),
    "sw1T": ([D, DI], F32), "sw3T": ([D, DI], F32), "sw2T": ([DI, D], F32),
    "WqkvT": ([D, 3072], F32), "WprojT": ([D, D], F32), "bproj128": ([128, 8], F32),
    "Wqkv6": ([6, D, 3072], F32), "Wo6": ([6, D, D], F32),
    "W1p": ([6, D, 2 * FFN_IP], F32), "b1p128": ([6, 128, 44], F32),
    "W2p": ([6, FFN_IP, D], F32), "b2128": ([6, 128, 8], F32),
    "flng128": ([6, 128, 8], F32), "flnb128": ([6, 128, 8], F32),
    "qs6": ([6, 128, 1], F32), "ks6": ([6, 128, 1], F32),
    "ident": ([128, 128], BF16), "identf": ([128, 128], F32),
    "ones2": ([128, 2], BF16),
    "sel2f": ([2, 128], F32),
    "masks": ([3, 128, 384], BF16),
    "csq": ([128, SEQ], BF16), "ssq": ([128, SEQ], BF16),
    "csk": ([128, SEQ], BF16), "ssk": ([128, SEQ], BF16),
}

def declare_tensors(nc, dbg=()):
    T = {}
    for name, (shape, dt) in INPUTS.items():
        T[name] = nc.dram_tensor(name, shape, dt, kind="ExternalInput").ap()
    for name, (shape, dt) in SCRATCH.items():
        kind = "ExternalOutput" if name in dbg else "Internal"
        T[name] = nc.dram_tensor(name, shape, dt, kind=kind).ap()
    T["out_fm"] = nc.dram_tensor("out_fm", [D, TOK], F32, kind="ExternalOutput").ap()
    return T


def emit_consts(ctx, T):
    g = DmaGroup(ctx, "ld")
    """Load resident const tiles; create ones tiles; zero scratch zero-cols."""
    nc = ctx.nc
    R = ctx.R
    for name, cols, dt in [
        ("ident", 128, BF16), ("identf", 128, F32), ("ones2", 2, BF16),
        ("masks", None, BF16),
        ("b_in128", 32, F32), ("convw128", 64, F32), ("convb128", 16, F32),
        ("b_dt128", 16, F32), ("dskip128", 16, F32), ("b_out128", 8, F32),
        ("lng128", 8, F32), ("lnb128", 8, F32), ("gate_b8", None, F32),
        ("bproj128", 8, F32),
        ("csq", SEQ, BF16), ("ssq", SEQ, BF16), ("csk", SEQ, BF16),
        ("ssk", SEQ, BF16),
    ]:
        if name == "masks":
            R["masks"] = ctx.sbuf("c_masks", [128, 3 * 384], BF16)
            for v in range(3):
                g.ld(R["masks"][:, v * 384:(v + 1) * 384], T["masks"][v])
        elif name == "gate_b8":
            R["gate_b8"] = ctx.sbuf("c_gate_b8", [1, 8], F32)
            g.ld(R["gate_b8"][:], T["gate_b8"])
        else:
            p = 2 if name == "sel2f" else 128
            R[name] = ctx.sbuf("c_" + name, [p, cols], dt)
            g.ld(R[name][:], T[name])
    R["sel2f"] = ctx.sbuf("c_sel2f", [2, 128], F32)
    g.ld(R["sel2f"][:], T["sel2f"])
    R["qs6"] = ctx.sbuf("c_qs6", [128, 6], F32)
    R["ks6"] = ctx.sbuf("c_ks6", [128, 6], F32)
    for i in range(6):
        g.ld(R["qs6"][:, i:i + 1], T["qs6"][i])
        g.ld(R["ks6"][:, i:i + 1], T["ks6"][i])
    for nm in ("b1p128", "b2128", "flng128", "flnb128"):
        cols = INPUTS[nm][0][2]
        R[nm] = ctx.sbuf("c_" + nm, [128, 6 * cols], F32)
        for i in range(6):
            g.ld(R[nm][:, i * cols:(i + 1) * cols], T[nm][i])
    ctx.constmap = {}
    cvals = [1e-5, 1.0, 1e-6, 1e-24]
    R["cconst"] = ctx.sbuf("c_cconst", [128, len(cvals)], F32)
    for i, v in enumerate(cvals):
        ctx.constmap[v] = i
        V(ctx, "memset", R["cconst"][:, i:i + 1], float(v))
    R["ones128b"] = ctx.sbuf("c_ones128b", [128, 1], BF16)
    R["ones128f"] = ctx.sbuf("c_ones128f", [128, 1], F32)
    R["ones1f"] = ctx.sbuf("c_ones1f", [1, 128], F32)
    R["zeros128"] = ctx.sbuf("c_zeros128", [128, WIN], F32)
    v0 = V(ctx, "memset", R["ones128b"][:], 1.0)
    v0 = V(ctx, "memset", R["ones128f"][:], 1.0)
    v0 = V(ctx, "memset", R["ones1f"][:], 1.0)
    v0 = V(ctx, "memset", R["zeros128"][:], 0.0)
    gs = DmaGroup(ctx, "st")
    ctx.wait(nc.gpsimd, "ev")
    for s in range(B_CORE):
        for ch in range(DI // 128):
            gs.st(T["xs_s"][ch * 128:(ch + 1) * 128, s * (SEQ + 3):s * (SEQ + 3) + 3],
                  R["zeros128"][:, 0:3])
            gs.st(T["xs2f"][ch * 128:(ch + 1) * 128, s * (SEQ + 3):s * (SEQ + 3) + 3],
                  R["zeros128"][:, 0:3])
        for ch in range(D // 128):
            gs.st(T["uA"][ch * 128:(ch + 1) * 128, s * (SEQ + 1):s * (SEQ + 1) + 1],
                  R["zeros128"][:, 0:1], allow_slow_non_contiguous=True)
            gs.st(T["uB"][ch * 128:(ch + 1) * 128, s * (SEQ + 1):s * (SEQ + 1) + 1],
                  R["zeros128"][:, 0:1], allow_slow_non_contiguous=True)
            gs.st(T["kscr"][ch * 128:(ch + 1) * 128, s * (WIN + SEQ):s * (WIN + SEQ) + WIN],
                  R["zeros128"][:, 0:WIN])
    zb = ctx.sbuf("c_zb", [128, 64], BF16)
    V(ctx, "memset", zb[:], 0.0)
    ctx.wait(nc.gpsimd, "ev")
    for b in range(B_CORE):
        for h in range(16):
            base = (b * 16 + h) * 18 * 128
            for j in range(2):
                gs.st(T["vscr"][base + j * 128: base + (j + 1) * 128, :], zb[:])
    ctx.barrier()


def load_weight_bf(ctx, name, dram_ap, K, F, colmap=None, stg=None):
    nc = ctx.nc
    nk = K // 128
    wb = ctx.sbuf(name + "_bf", [128, nk * F], BF16)
    if stg is None:
        stg = [ctx.sbuf(name + "_stg0", [128, F], F32),
               ctx.sbuf(name + "_stg1", [128, F], F32)]
    stream_weight_bf(ctx, wb, dram_ap, K, F, stg, colmap)
    return wb


def stream_weight_bf(ctx, wb, dram_ap, K, F, stg, colmap=None):
    nc = ctx.nc
    nk = K // 128
    c0, cw = (0, F) if colmap is None else colmap
    evs = {}
    ev0 = ctx.cnt["ev"]
    for k in range(nk):
        ctx.wait(nc.sync, "ev", evs[k - 2] if k >= 2 else ev0)
        g = DmaGroup(ctx, "ldB")
        g.ld(stg[k % 2][:, 0:cw], dram_ap[k * 128:(k + 1) * 128, c0:c0 + cw])
        g.wait(nc.vector)
        evs[k] = V(ctx, "tensor_copy", wb[:, k * F:k * F + cw], stg[k % 2][:, 0:cw])


def cbias(ctx, v, p=128):
    i = ctx.constmap[v]
    return ctx.R["cconst"][0:p, i:i + 1]


def V(ctx, name, *a, **k):
    ctx.wait(ctx.nc.vector, "ev")
    return ctx.inc(getattr(ctx.nc.vector, name)(*a, **k), "ev")


def S(ctx, name, *a, **k):
    ctx.wait(ctx.nc.scalar, "ac")
    return ctx.inc(getattr(ctx.nc.scalar, name)(*a, **k), "ac")


def P(ctx, name, *a, **k):
    ctx.wait(ctx.nc.gpsimd, "pp")
    return ctx.inc(getattr(ctx.nc.gpsimd, name)(*a, **k), "pp")


def bcast_cols(ctx, psum_dst, row_f32):
    nc = ctx.nc
    return nc.tensor.matmul(psum_dst, ctx.R["ones1f"][:], row_f32, start=True, stop=True)



def emit_outer_ln(ctx, T, src, dst_bf=None, dst_f32=None, dst_zerocol=False, pfx="ln"):
    nc = ctx.nc
    ctx.push_frame()
    nk = D // 128
    xb = [ctx.sbuf(f"{pfx}_x{p}", [128, nk * TT], F32) for p in range(2)]
    cb = [ctx.sbuf(f"{pfx}_c{p}", [128, nk * TT], BF16) for p in range(2)]
    sq = [ctx.sbuf(f"{pfx}_q{p}", [128, nk * TT], BF16) for p in range(2)]
    ot = [ctx.sbuf(f"{pfx}_o{p}", [128, nk * TT], F32) for p in range(2)]
    ob = [ctx.sbuf(f"{pfx}_ob{p}", [128, nk * TT], BF16) for p in range(2)]
    sc = ctx.sbuf(f"{pfx}_sc", [1, 8 * TT], F32)
    rb = [ctx.sbuf(f"{pfx}_rb{p}", [128, TT], F32) for p in range(2)]
    psA = [ctx.psum(f"{pfx}_psA{p}", [1, TT]) for p in range(2)]
    psB = [ctx.psum(f"{pfx}_psB{p}", [1, TT]) for p in range(2)]
    psM = [ctx.psum(f"{pfx}_psM{p}", [128, TT]) for p in range(2)]
    psR = [ctx.psum(f"{pfx}_psR{p}", [128, TT]) for p in range(2)]
    done_ev = {}
    st_grp = {}
    done_sq = {}
    ones = ctx.R["ones128b"]
    for t in range(NTT):
        p = t % 2
        if t >= 2:
            ctx.wait(nc.sync, "ev", done_ev[t - 2])
            ctx.wait(nc.sync, "ac", done_sq[t - 2])
        g = DmaGroup(ctx, "ld")
        for k in range(nk):
            g.ld(xb[p][:, k * TT:(k + 1) * TT],
                 src[k * 128:(k + 1) * 128, t * TT:(t + 1) * TT])
        g.wait(nc.vector)
        for k in range(nk):
            V(ctx, "tensor_copy", cb[p][:, k * TT:(k + 1) * TT],
                                          xb[p][:, k * TT:(k + 1) * TT])
        g.wait(nc.scalar)
        for k in range(nk):
            S(ctx, "activation", sq[p][:, k * TT:(k + 1) * TT],
                                         xb[p][:, k * TT:(k + 1) * TT],
                                         AF.Square, bias=0.0, scale=1.0)
        done_sq[t] = ctx.cnt["ac"]
        ctx.wait(nc.tensor, "ev")
        ctx.wait(nc.tensor, "ac")
        for k in range(nk):
            nc.tensor.matmul(psA[p][:], ones[:], cb[p][:, k * TT:(k + 1) * TT],
                             start=(k == 0), stop=(k == nk - 1))
        last = None
        for k in range(nk):
            last = nc.tensor.matmul(psB[p][:], ones[:], sq[p][:, k * TT:(k + 1) * TT],
                                    start=(k == 0), stop=(k == nk - 1))
        ctx.inc(last, "mm")
        mu = sc[:, (4 * p + 0) * TT:(4 * p + 1) * TT]
        vv = sc[:, (4 * p + 1) * TT:(4 * p + 2) * TT]
        ms = sc[:, (4 * p + 2) * TT:(4 * p + 3) * TT]
        rs = sc[:, (4 * p + 3) * TT:(4 * p + 4) * TT]
        ctx.wait(nc.scalar, "mm")
        S(ctx, "activation", mu, psA[p][:], AF.Copy, bias=0.0, scale=1.0 / D)
        S(ctx, "activation", vv, psB[p][:], AF.Copy, bias=0.0, scale=1.0 / D)
        S(ctx, "activation", ms, mu, AF.Square, bias=0.0, scale=1.0)
        ctx.wait(nc.vector, "ac")
        V(ctx, "tensor_tensor", vv, vv, ms, ALU.subtract)
        ctx.wait(nc.scalar, "ev")
        S(ctx, "activation", vv, vv, AF.Ln, bias=cbias(ctx, 1e-5, 1), scale=1.0)
        S(ctx, "activation", rs, vv, AF.Exp, bias=0.0, scale=-0.5)
        ctx.wait(nc.tensor, "ac")
        bcast_cols(ctx, psM[p][:], mu)
        ctx.inc(bcast_cols(ctx, psR[p][:], rs), "mm")
        ctx.wait(nc.vector, "mm")
        if t >= 2:
            st_grp[t - 2].wait(nc.vector)
        V(ctx, "tensor_copy", rb[p][:], psR[p][:])
        for k in range(nk):
            V(ctx, "tensor_tensor", ot[p][:, k * TT:(k + 1) * TT],
                                            xb[p][:, k * TT:(k + 1) * TT],
                                            psM[p][:], ALU.subtract)
            V(ctx, "tensor_tensor", ot[p][:, k * TT:(k + 1) * TT],
                                            ot[p][:, k * TT:(k + 1) * TT],
                                            rb[p][:], ALU.mult)
            if dst_bf is not None:
                V(ctx, "tensor_copy", ob[p][:, k * TT:(k + 1) * TT],
                                              ot[p][:, k * TT:(k + 1) * TT])
        done_ev[t] = ctx.cnt["ev"]
        ctx.wait(nc.gpsimd, "ev")
        gs = DmaGroup(ctx, "st")
        s, t0 = t // TPS, (t % TPS) * TT
        for k in range(nk):
            if dst_f32 is not None:
                col = s * (SEQ + 1) + 1 + t0 if dst_zerocol else t * TT
                gs.st(dst_f32[k * 128:(k + 1) * 128, col:col + TT],
                      ot[p][:, k * TT:(k + 1) * TT])
            if dst_bf is not None:
                gs.st(dst_bf[k * 128:(k + 1) * 128, t * TT:(t + 1) * TT],
                      ob[p][:, k * TT:(k + 1) * TT])
        st_grp[t] = gs
    ctx.barrier()
    ctx.pop_frame()



def emit_l0_inproj(ctx, T):
    nc = ctx.nc
    ctx.push_frame()
    nk = D // 128
    nf = 4096 // 128
    NS = 6
    wb = load_weight_bf(ctx, "w_in", T["w_in"], D, 4096)
    rhs = [ctx.sbuf(f"ip_r{p}", [128, nk * TT], BF16) for p in range(2)]
    ob = ctx.sbuf("ip_ob", [128, NS * TT], F32)
    ps = [ctx.psum(f"ip_ps{p}", [128, TT]) for p in range(4)]
    done_mm = {}
    slot_st = {}
    ac_base = ctx.cnt["ac"]
    gidx = 0
    for t in range(NTT):
        p = t % 2
        if t >= 2:
            ctx.wait(nc.sync, "mm", done_mm[t - 2])
        g = DmaGroup(ctx, "ld")
        for k in range(nk):
            g.ld(rhs[p][:, k * TT:(k + 1) * TT],
                 T["xn0b"][k * 128:(k + 1) * 128, t * TT:(t + 1) * TT])
        g.wait(nc.tensor)
        if t == 0:
            ctx.wait(nc.tensor, "ev")
        s, t0 = t // TPS, (t % TPS) * TT
        for f in range(nf):
            bank = ps[gidx % 4]
            ctx.wait(nc.tensor, "ac", max(0, ac_base + gidx - 3))
            last = None
            for k in range(nk):
                last = nc.tensor.matmul(bank[:], wb[:, k * 4096 + f * 128: k * 4096 + (f + 1) * 128],
                                        rhs[p][:, k * TT:(k + 1) * TT],
                                        start=(k == 0), stop=(k == nk - 1))
            ctx.inc(last, "mm")
            slot = gidx % NS
            ctx.wait(nc.scalar, "mm")
            if gidx >= NS:
                slot_st[gidx - NS].wait(nc.scalar)
            S(ctx, "activation", ob[:, slot * TT:(slot + 1) * TT], bank[:],
                                         AF.Silu, bias=ctx.R["b_in128"][:, f:f + 1],
                                         scale=1.0)
            ctx.wait(nc.gpsimd, "ac")
            gs = DmaGroup(ctx, "st")
            if f < 16:
                col = s * (SEQ + 3) + 3 + t0
                gs.st(T["xs_s"][f * 128:(f + 1) * 128, col:col + TT],
                      ob[:, slot * TT:(slot + 1) * TT])
            else:
                fr = f - 16
                gs.st(T["sres"][fr * 128:(fr + 1) * 128, t * TT:(t + 1) * TT],
                      ob[:, slot * TT:(slot + 1) * TT])
            slot_st[gidx] = gs
            gidx += 1
        done_mm[t] = ctx.cnt["mm"]
    ctx.barrier()
    ctx.pop_frame()



def emit_l0_conv(ctx, T):
    nc = ctx.nc
    ctx.push_frame()
    xin = [ctx.sbuf(f"cv_x{p}", [128, SEQ + 3], F32) for p in range(2)]
    acc = [ctx.sbuf(f"cv_a{p}", [128, SEQ], F32) for p in range(2)]
    oo = [ctx.sbuf(f"cv_o{p}", [128, SEQ], F32) for p in range(2)]
    ob = [ctx.sbuf(f"cv_ob{p}", [128, SEQ], BF16) for p in range(2)]
    done_ev = {}
    done_ac = {}
    st_grp = {}
    cw = ctx.R["convw128"]
    u = 0
    for s in range(B_CORE):
        for c in range(DI // 128):
            p = u % 2
            if u >= 2:
                ctx.wait(nc.sync, "ev", done_ev[u - 2])
                ctx.wait(nc.sync, "ac", done_ac[u - 2])
            g = DmaGroup(ctx, "ld")
            g.ld(xin[p][:], T["xs_s"][c * 128:(c + 1) * 128,
                                      s * (SEQ + 3):(s + 1) * (SEQ + 3)])
            g.wait(nc.vector)
            if u >= 2:
                ctx.wait(nc.vector, "ac", done_ac[u - 2])
            V(ctx, "tensor_scalar_mul", acc[p][:], xin[p][:, 0:SEQ], cw[:, c * 4:c * 4 + 1])
            for k in range(1, 4):
                V(ctx, "scalar_tensor_tensor", acc[p][:], xin[p][:, k:k + SEQ],
                  cw[:, c * 4 + k:c * 4 + k + 1], acc[p][:], ALU.mult, ALU.add)
            done_ev[u] = ctx.cnt["ev"]
            ctx.wait(nc.scalar, "ev")
            if u >= 2:
                ctx.wait(nc.scalar, "ev", done_ev[u - 2])
                st_grp[u - 2].wait(nc.scalar)
            S(ctx, "activation", oo[p][:], acc[p][:], AF.Silu,
              bias=ctx.R["convb128"][:, c:c + 1], scale=1.0)
            done_ac[u] = ctx.cnt["ac"]
            ctx.wait(nc.vector, "ac")
            if u >= 2:
                st_grp[u - 2].wait(nc.vector)
            V(ctx, "tensor_copy", ob[p][:], oo[p][:])
            ctx.wait(nc.gpsimd, "ev")
            gs = DmaGroup(ctx, "st")
            gs.st(T["xs2f"][c * 128:(c + 1) * 128,
                            s * (SEQ + 3) + 3:(s + 1) * (SEQ + 3)], oo[p][:])
            gs.st(T["xs2b"][c * 128:(c + 1) * 128, s * SEQ:(s + 1) * SEQ], ob[p][:])
            st_grp[u] = gs
            u += 1
    ctx.barrier()
    ctx.pop_frame()



def emit_l0_dt(ctx, T):
    nc = ctx.nc
    ctx.push_frame()
    nk = DI // 128
    nf = DI // 128
    wb = load_weight_bf(ctx, "w_dt", T["w_dt"], DI, DI)
    rhs = [ctx.sbuf(f"dt_r{p}", [128, nk * TT], BF16) for p in range(2)]
    xfr = ctx.sbuf("dt_xf", [128, 4 * TT], F32)
    t1 = ctx.sbuf("dt_t1", [128, TT], F32)
    t2 = ctx.sbuf("dt_t2", [128, TT], F32)
    NS = 4
    oa = ctx.sbuf("dt_oa", [128, NS * TT], F32)
    obb = ctx.sbuf("dt_ob", [128, NS * TT], F32)
    ps = [ctx.psum(f"dt_ps{p}", [128, TT]) for p in range(4)]
    done_mm = {}
    slot_st = {}
    pr_ac = {}
    xf_ev = {}
    gidx = 0
    for t in range(NTT):
        p = t % 2
        s, t0 = t // TPS, (t % TPS) * TT
        if t >= 2:
            ctx.wait(nc.sync, "mm", done_mm[t - 2])
        g = DmaGroup(ctx, "ld")
        for k in range(nk):
            g.ld(rhs[p][:, k * TT:(k + 1) * TT],
                 T["xs2b"][k * 128:(k + 1) * 128, t * TT:(t + 1) * TT])
        g.wait(nc.tensor)
        if t == 0:
            ctx.wait(nc.tensor, "ev")
        for f in range(nf):
            bank = ps[gidx % 4]
            xslot = gidx % 4
            if gidx >= 4:
                ctx.wait(nc.sync, "ev", xf_ev[gidx - 4])
            gx = DmaGroup(ctx, "ldC")
            gx.ld(xfr[:, xslot * TT:(xslot + 1) * TT],
                  T["xs2f"][f * 128:(f + 1) * 128,
                            s * (SEQ + 3) + 3 + t0:s * (SEQ + 3) + 3 + t0 + TT])
            if gidx >= 4:
                ctx.wait(nc.tensor, "ac", pr_ac[gidx - 4])
            last = None
            for k in range(nk):
                last = nc.tensor.matmul(bank[:], wb[:, k * DI + f * 128: k * DI + (f + 1) * 128],
                                        rhs[p][:, k * TT:(k + 1) * TT],
                                        start=(k == 0), stop=(k == nk - 1))
            ctx.inc(last, "mm")
            slot = gidx % NS
            bdt = ctx.R["b_dt128"][:, f:f + 1]
            ctx.wait(nc.scalar, "mm")
            ctx.wait(nc.scalar, "ev")
            S(ctx, "activation", t1[:], bank[:], AF.Abs, bias=bdt, scale=1.0)
            S(ctx, "activation", t1[:], t1[:], AF.Exp, bias=0.0, scale=-1.0)
            S(ctx, "activation", t1[:], t1[:], AF.Ln, bias=cbias(ctx, 1.0), scale=1.0)
            S(ctx, "activation", t2[:], bank[:], AF.Relu, bias=bdt, scale=1.0)
            pr_ac[gidx] = ctx.cnt["ac"]
            ctx.wait(nc.vector, "ac")
            V(ctx, "tensor_tensor", t2[:], t2[:], t1[:], ALU.add)
            ctx.wait(nc.scalar, "ev")
            S(ctx, "activation", t1[:], t2[:], AF.Exp, bias=0.0, scale=-1.0)
            if gidx >= NS:
                slot_st[gidx - NS].wait(nc.vector)
            ctx.wait(nc.vector, "ac")
            V(ctx, "tensor_scalar_add", t2[:], t1[:], 1.0)
            V(ctx, "reciprocal", t2[:], t2[:])
            V(ctx, "tensor_tensor", oa[:, slot * TT:(slot + 1) * TT], t1[:], t2[:], ALU.mult)
            gx.wait(nc.vector)
            V(ctx, "tensor_tensor", obb[:, slot * TT:(slot + 1) * TT], t2[:],
              xfr[:, xslot * TT:(xslot + 1) * TT], ALU.mult)
            xf_ev[gidx] = ctx.cnt["ev"]
            ctx.wait(nc.gpsimd, "ev")
            gs = DmaGroup(ctx, "st")
            gs.st(T["sa"][f * 128:(f + 1) * 128, t * TT:(t + 1) * TT],
                  oa[:, slot * TT:(slot + 1) * TT])
            gs.st(T["sb"][f * 128:(f + 1) * 128, t * TT:(t + 1) * TT],
                  obb[:, slot * TT:(slot + 1) * TT])
            slot_st[gidx] = gs
            gidx += 1
        done_mm[t] = ctx.cnt["mm"]
    ctx.barrier()
    ctx.pop_frame()



def emit_l0_scan(ctx, T):
    nc = ctx.nc
    ctx.push_frame()
    sa = [ctx.sbuf(f"sc_a{p}", [128, SEQ], F32) for p in range(2)]
    sb = [ctx.sbuf(f"sc_b{p}", [128, SEQ], F32) for p in range(2)]
    xf = [ctx.sbuf(f"sc_x{p}", [128, SEQ], F32) for p in range(2)]
    sr = [ctx.sbuf(f"sc_r{p}", [128, SEQ], F32) for p in range(2)]
    yb = [ctx.sbuf(f"sc_y{p}", [128, SEQ], BF16) for p in range(2)]
    done_ev = {}
    st_grp = {}
    u = 0
    for s in range(B_CORE):
        for c in range(DI // 128):
            p = u % 2
            if u >= 2:
                ctx.wait(nc.sync, "ev", done_ev[u - 2])
            g = DmaGroup(ctx, "ld")
            g.ld(sa[p][:], T["sa"][c * 128:(c + 1) * 128, s * SEQ:(s + 1) * SEQ])
            g.ld(sb[p][:], T["sb"][c * 128:(c + 1) * 128, s * SEQ:(s + 1) * SEQ])
            g.ld(xf[p][:], T["xs2f"][c * 128:(c + 1) * 128,
                                     s * (SEQ + 3) + 3:(s + 1) * (SEQ + 3)])
            g.ld(sr[p][:], T["sres"][c * 128:(c + 1) * 128, s * SEQ:(s + 1) * SEQ])
            g.wait(nc.vector)
            V(ctx, "memset", sa[p][:, 0:1], 0.0)
            V(ctx, "tensor_copy", sb[p][:, 0:1], xf[p][:, 0:1])
            V(ctx, "tensor_tensor_scan", sb[p][:], sa[p][:], sb[p][:], 0.0,
              ALU.mult, ALU.add)
            V(ctx, "scalar_tensor_tensor", sb[p][:], xf[p][:],
              ctx.R["dskip128"][:, c:c + 1], sb[p][:], ALU.mult, ALU.add)
            if u >= 2:
                st_grp[u - 2].wait(nc.vector)
            V(ctx, "tensor_tensor", yb[p][:], sb[p][:], sr[p][:], ALU.mult)
            done_ev[u] = ctx.cnt["ev"]
            ctx.wait(nc.gpsimd, "ev")
            gs = DmaGroup(ctx, "st")
            gs.st(T["yg"][c * 128:(c + 1) * 128, s * SEQ:(s + 1) * SEQ], yb[p][:])
            st_grp[u] = gs
            u += 1
    ctx.barrier()
    ctx.pop_frame()



def emit_l0_outproj(ctx, T):
    nc = ctx.nc
    ctx.push_frame()
    nk = DI // 128
    nf = D // 128
    wb = load_weight_bf(ctx, "w_out", T["w_out"], DI, D)
    rhs = [ctx.sbuf(f"op_r{p}", [128, nk * TT], BF16) for p in range(2)]
    xnr = ctx.sbuf("op_xn", [128, 4 * TT], F32)
    xt = [ctx.sbuf(f"op_x{p}", [128, nf * TT], F32) for p in range(2)]
    h2 = [ctx.sbuf(f"op_h{p}", [128, nf * TT], F32) for p in range(2)]
    sq = [ctx.sbuf(f"op_q{p}", [128, nf * TT], BF16) for p in range(2)]
    sc = ctx.sbuf("op_sc", [1, 8 * TT], F32)
    rb = [ctx.sbuf(f"op_rb{p}", [128, TT], F32) for p in range(2)]
    ps = [ctx.psum(f"op_ps{p}", [128, TT]) for p in range(2)]
    psA = [ctx.psum("op_psA", [1, TT])] * 2
    psB = [ctx.psum("op_psB", [1, TT])] * 2
    psM = [ctx.psum("op_psM", [128, TT])] * 2
    psR = [ctx.psum("op_psR", [128, TT])] * 2
    ones = ctx.R["ones128b"]
    done_ev = {}
    done_mm = {}
    done_sq = {}
    st_grp = {}
    xn_ev = {}
    for t in range(NTT):
        p = t % 2
        if t >= 2:
            ctx.wait(nc.sync, "mm", done_mm[t - 2])
            ctx.wait(nc.sync, "ev", done_ev[t - 2])
            ctx.wait(nc.sync, "ac", done_sq[t - 2])
        g = DmaGroup(ctx, "ld")
        for k in range(nk):
            g.ld(rhs[p][:, k * TT:(k + 1) * TT],
                 T["yg"][k * 128:(k + 1) * 128, t * TT:(t + 1) * TT])
        for f in range(nf):
            g.ld(xt[p][:, f * TT:(f + 1) * TT],
                 T["x_fm"][f * 128:(f + 1) * 128, t * TT:(t + 1) * TT])
        g.wait(nc.tensor)
        if t == 0:
            ctx.wait(nc.tensor, "ev")
        g.wait(nc.vector)
        g.wait(nc.scalar)
        if t >= 2:
            st_grp[t - 2].wait(nc.vector)
        for f in range(nf):
            bank = ps[f % 2]
            gidx5 = t * nf + f
            xslot = gidx5 % 4
            if gidx5 >= 4:
                ctx.wait(nc.sync, "ev", xn_ev[gidx5 - 4])
            gx = DmaGroup(ctx, "ldC")
            gx.ld(xnr[:, xslot * TT:(xslot + 1) * TT],
                  T["xn0f"][f * 128:(f + 1) * 128, t * TT:(t + 1) * TT])
            if f >= 2:
                ctx.wait(nc.tensor, "ev")
            last = None
            for k in range(nk):
                last = nc.tensor.matmul(bank[:], wb[:, k * D + f * 128: k * D + (f + 1) * 128],
                                        rhs[p][:, k * TT:(k + 1) * TT],
                                        start=(k == 0), stop=(k == nk - 1))
            ctx.inc(last, "mm")
            ctx.wait(nc.vector, "mm")
            gx.wait(nc.vector)
            V(ctx, "scalar_tensor_tensor", h2[p][:, f * TT:(f + 1) * TT], bank[:],
              ctx.R["b_out128"][:, f:f + 1], xnr[:, xslot * TT:(xslot + 1) * TT],
              ALU.add, ALU.add)
            xn_ev[gidx5] = ctx.cnt["ev"]
            ctx.wait(nc.scalar, "ev")
            S(ctx, "activation", sq[p][:, f * TT:(f + 1) * TT],
              h2[p][:, f * TT:(f + 1) * TT], AF.Square, bias=0.0, scale=1.0)
        done_sq[t] = ctx.cnt["ac"]
        ctx.wait(nc.tensor, "ev")
        ctx.wait(nc.tensor, "ac")
        for k in range(nf):
            nc.tensor.matmul(psA[p][:], ctx.R["ones128f"][:], h2[p][:, k * TT:(k + 1) * TT],
                             start=(k == 0), stop=(k == nf - 1))
        last = None
        for k in range(nf):
            last = nc.tensor.matmul(psB[p][:], ones[:], sq[p][:, k * TT:(k + 1) * TT],
                                    start=(k == 0), stop=(k == nf - 1))
        ctx.inc(last, "mm")
        mu = sc[:, (4 * p + 0) * TT:(4 * p + 1) * TT]
        vv = sc[:, (4 * p + 1) * TT:(4 * p + 2) * TT]
        ms = sc[:, (4 * p + 2) * TT:(4 * p + 3) * TT]
        rs = sc[:, (4 * p + 3) * TT:(4 * p + 4) * TT]
        ctx.wait(nc.scalar, "mm")
        S(ctx, "activation", mu, psA[p][:], AF.Copy, bias=0.0, scale=1.0 / D)
        S(ctx, "activation", vv, psB[p][:], AF.Copy, bias=0.0, scale=1.0 / D)
        S(ctx, "activation", ms, mu, AF.Square, bias=0.0, scale=1.0)
        ctx.wait(nc.vector, "ac")
        V(ctx, "tensor_tensor", vv, vv, ms, ALU.subtract)
        ctx.wait(nc.scalar, "ev")
        S(ctx, "activation", vv, vv, AF.Ln, bias=cbias(ctx, 1e-5, 1), scale=1.0)
        S(ctx, "activation", rs, vv, AF.Exp, bias=0.0, scale=-0.5)
        ctx.wait(nc.tensor, "ac")
        bcast_cols(ctx, psM[p][:], mu)
        ctx.inc(bcast_cols(ctx, psR[p][:], rs), "mm")
        ctx.wait(nc.vector, "mm")
        V(ctx, "tensor_copy", rb[p][:], psR[p][:])
        for f in range(nf):
            V(ctx, "tensor_tensor", h2[p][:, f * TT:(f + 1) * TT],
              h2[p][:, f * TT:(f + 1) * TT], psM[p][:], ALU.subtract)
            V(ctx, "tensor_tensor", h2[p][:, f * TT:(f + 1) * TT],
              h2[p][:, f * TT:(f + 1) * TT], rb[p][:], ALU.mult)
        for f in range(nf):
            V(ctx, "tensor_scalar", h2[p][:, f * TT:(f + 1) * TT],
              h2[p][:, f * TT:(f + 1) * TT], ctx.R["lng128"][:, f:f + 1],
              ctx.R["lnb128"][:, f:f + 1], ALU.mult, ALU.add)
        for f in range(nf):
            V(ctx, "tensor_tensor", h2[p][:, f * TT:(f + 1) * TT],
              h2[p][:, f * TT:(f + 1) * TT], xt[p][:, f * TT:(f + 1) * TT], ALU.add)
        done_ev[t] = ctx.cnt["ev"]
        done_mm[t] = ctx.cnt["mm"]
        ctx.wait(nc.gpsimd, "ev")
        gs = DmaGroup(ctx, "st")
        for f in range(nf):
            gs.st(T["p1"][f * 128:(f + 1) * 128, t * TT:(t + 1) * TT],
                  h2[p][:, f * TT:(f + 1) * TT])
        st_grp[t] = gs
    ctx.barrier()
    ctx.pop_frame()



def emit_moe_gate(ctx, T):
    nc = ctx.nc
    ctx.push_frame()
    nk = D // 128
    wgb = load_weight_bf(ctx, "gate_wT", T["gate_wT"], D, 8)
    rhs = [ctx.sbuf(f"gt_r{p}", [128, nk * TT], BF16) for p in range(2)]
    e8 = ctx.sbuf("gt_e8", [128, 8], F32)
    s8 = ctx.sbuf("gt_s8", [128, 8], F32)
    t8 = ctx.sbuf("gt_t8", [128, 8], F32)
    g8 = ctx.sbuf("gt_g8", [128, 8], F32)
    m8 = ctx.sbuf("gt_m8", [128, 4], F32)
    gb = ctx.sbuf("gt_gb", [128, 8], F32)
    go = [ctx.sbuf(f"gt_go{p}", [8, 4 * 128], F32) for p in range(2)]
    ps = [ctx.psum(f"gt_ps{q}", [128, 8]) for q in range(2)]
    pst = ctx.psum("gt_pst", [8, 128])
    psg = ctx.psum("gt_psg", [128, 8])
    ctx.inc(nc.tensor.matmul(psg[:], ctx.R["ones1f"][:], ctx.R["gate_b8"][:],
                             start=True, stop=True), "mm")
    ctx.wait(nc.vector, "mm")
    V(ctx, "tensor_copy", gb[:], psg[:])
    done_mm = {}
    st_grp = {}
    for t in range(NTT):
        p = t % 2
        if t >= 2:
            ctx.wait(nc.sync, "mm", done_mm[t - 2])
        g = DmaGroup(ctx, "ld")
        for k in range(nk):
            g.ld(rhs[p][:, k * TT:(k + 1) * TT],
                 T["xn0b"][k * 128:(k + 1) * 128, t * TT:(t + 1) * TT])
        g.wait(nc.tensor)
        if t == 0:
            ctx.wait(nc.tensor, "ev")
        if t >= 2:
            st_grp[t - 2].wait(nc.vector)
        for c4 in range(4):
            q = c4 % 2
            ctx.wait(nc.tensor, "ev")
            last = None
            for k in range(nk):
                last = nc.tensor.matmul(ps[q][:],
                                        rhs[p][:, k * TT + c4 * 128:k * TT + (c4 + 1) * 128],
                                        wgb[:, k * 8:(k + 1) * 8],
                                        start=(k == 0), stop=(k == nk - 1))
            ctx.inc(last, "mm")
            ctx.wait(nc.vector, "mm")
            V(ctx, "tensor_reduce", m8[:, 0:1], ps[q][:], axis=AX.X, op=ALU.max,
              negate=True)
            ctx.wait(nc.scalar, "ev")
            S(ctx, "activation", e8[:], ps[q][:], AF.Exp, bias=m8[:, 0:1], scale=1.0,
              accum_out=m8[:, 1:2])
            ctx.wait(nc.vector, "ac")
            V(ctx, "reciprocal", m8[:, 1:2], m8[:, 1:2])
            V(ctx, "tensor_scalar_mul", s8[:], e8[:], m8[:, 1:2])
            V(ctx, "tensor_tensor", t8[:], s8[:], gb[:], ALU.add)
            V(ctx, "tensor_reduce", m8[:, 2:3], t8[:], axis=AX.X, op=ALU.max)
            V(ctx, "tensor_single_scalar", g8[:], t8[:], m8[:, 2:3], ALU.is_ge)
            V(ctx, "scalar_tensor_tensor", g8[:], g8[:], -3.4e38, t8[:],
              ALU.mult, ALU.add)
            V(ctx, "tensor_reduce", m8[:, 3:4], g8[:], axis=AX.X, op=ALU.max)
            V(ctx, "tensor_single_scalar", g8[:], t8[:], m8[:, 3:4], ALU.is_ge)
            V(ctx, "tensor_tensor", g8[:], g8[:], s8[:], ALU.mult)
            ctx.wait(nc.tensor, "ev")
            ctx.inc(nc.tensor.transpose(pst[:], g8[:], ctx.R["identf"][:]), "mm")
            ctx.wait(nc.vector, "mm")
            V(ctx, "tensor_copy", go[p][:, c4 * 128:(c4 + 1) * 128], pst[:])
        done_mm[t] = ctx.cnt["mm"]
        ctx.wait(nc.gpsimd, "ev")
        gs = DmaGroup(ctx, "st")
        gs.st(T["gfm"][:, t * TT:(t + 1) * TT], go[p][:])
        st_grp[t] = gs
    ctx.barrier()
    ctx.pop_frame()



def emit_moe_expert(ctx, T, e):
    nc = ctx.nc
    ctx.push_frame()
    shared = (e == -1)
    NI = 16 if shared else 8
    nk = D // 128
    nf = D // 128
    FI = DI if shared else 1024
    nm = f"mx{e}_"
    stg = [ctx.sbuf(nm + "stg0", [128, FI], F32), ctx.sbuf(nm + "stg1", [128, FI], F32)]
    if shared:
        w1b = load_weight_bf(ctx, "sw1T", T["sw1T"], D, DI, stg=stg)
        w3b = load_weight_bf(ctx, "sw3T", T["sw3T"], D, DI, stg=stg)
        w2b = load_weight_bf(ctx, "sw2T", T["sw2T"], DI, D, stg=stg)
    else:
        w1b = load_weight_bf(ctx, f"w1T{e}", T["w1T"][e], D, 1024, stg=stg)
        w3b = load_weight_bf(ctx, f"w3T{e}", T["w3T"][e], D, 1024, stg=stg)
        w2b = load_weight_bf(ctx, f"w2T{e}", T["w2T"][e], 1024, D, stg=stg)
    rhs = [ctx.sbuf(nm + f"r{p}", [128, nk * TT], BF16) for p in range(2)]
    hsc = ctx.sbuf(nm + "h", [128, NI * TT], BF16)
    x1t = ctx.sbuf(nm + "x1", [128, nf * TT], F32)
    s1 = ctx.sbuf(nm + "s1", [128, TT], F32)
    s2 = ctx.sbuf(nm + "s2", [128, TT], F32)
    gbs = ctx.sbuf(nm + "gb", [128, TT], F32)
    grow = ctx.sbuf(nm + "gr", [1, 2 * TT], F32)
    ph = [ctx.psum(nm + f"ph{q}", [128, TT]) for q in range(4)]
    po = [ctx.psum(nm + f"po{q}", [128, TT]) for q in range(2)]
    psg = ctx.psum(nm + "psg", [128, TT])
    done_mm = {}
    st_grp = {}
    gl_ev = {}
    for t in range(NTT):
        p = t % 2
        if t >= 2:
            ctx.wait(nc.sync, "mm", done_mm[t - 2])
        g = DmaGroup(ctx, "ld")
        for k in range(nk):
            g.ld(rhs[p][:, k * TT:(k + 1) * TT],
                 T["xn0b"][k * 128:(k + 1) * 128, t * TT:(t + 1) * TT])
        if not shared:
            if t >= 2:
                ctx.wait(nc.sync, "ev", gl_ev[t - 2])
            gg = DmaGroup(ctx, "ldC")
            gg.ld(grow[:, (t % 2) * TT:(t % 2 + 1) * TT],
                  T["gfm"][e:e + 1, t * TT:(t + 1) * TT])
        g.wait(nc.tensor)
        if t == 0:
            ctx.wait(nc.tensor, "ev")
        if not shared:
            gg.wait(nc.tensor)
            ctx.wait(nc.tensor, "ev")
            ctx.inc(bcast_cols(ctx, psg[:], grow[:, (t % 2) * TT:(t % 2 + 1) * TT]), "mm")
            ctx.wait(nc.vector, "mm")
            V(ctx, "tensor_copy", gbs[:], psg[:])
            gl_ev[t] = ctx.cnt["ev"]
        for ic in range(NI):
            bh1 = ph[(ic % 2) * 2]
            bh3 = ph[(ic % 2) * 2 + 1]
            ctx.wait(nc.tensor, "ev")
            ctx.wait(nc.tensor, "ac")
            for k in range(nk):
                nc.tensor.matmul(bh1[:], w1b[:, k * FI + ic * 128:k * FI + (ic + 1) * 128],
                                 rhs[p][:, k * TT:(k + 1) * TT],
                                 start=(k == 0), stop=(k == nk - 1))
            last = None
            for k in range(nk):
                last = nc.tensor.matmul(bh3[:], w3b[:, k * FI + ic * 128:k * FI + (ic + 1) * 128],
                                        rhs[p][:, k * TT:(k + 1) * TT],
                                        start=(k == 0), stop=(k == nk - 1))
            ctx.inc(last, "mm")
            ctx.wait(nc.scalar, "mm")
            ctx.wait(nc.scalar, "ev")
            S(ctx, "activation", s1[:], bh1[:], AF.Silu, bias=0.0, scale=1.0)
            ctx.wait(nc.vector, "ac")
            V(ctx, "tensor_tensor", s2[:], s1[:], bh3[:], ALU.mult)
            if shared:
                V(ctx, "tensor_copy", hsc[:, ic * TT:(ic + 1) * TT], s2[:])
            else:
                V(ctx, "tensor_tensor", hsc[:, ic * TT:(ic + 1) * TT], s2[:],
                  gbs[:], ALU.mult)
        if t >= 1:
            st_grp[t - 1].wait(nc.sync)
        gx = DmaGroup(ctx, "ldC")
        for f in range(nf):
            gx.ld(x1t[:, f * TT:(f + 1) * TT],
                  T["p1" if shared else "x1"][f * 128:(f + 1) * 128, t * TT:(t + 1) * TT])
        gx.wait(nc.vector)
        for f in range(nf):
            bank = po[f % 2]
            ctx.wait(nc.tensor, "ev")
            last = None
            for k in range(NI):
                last = nc.tensor.matmul(bank[:], w2b[:, k * D + f * 128:k * D + (f + 1) * 128],
                                        hsc[:, k * TT:(k + 1) * TT],
                                        start=(k == 0), stop=(k == NI - 1))
            ctx.inc(last, "mm")
            ctx.wait(nc.vector, "mm")
            V(ctx, "tensor_tensor", x1t[:, f * TT:(f + 1) * TT],
              x1t[:, f * TT:(f + 1) * TT], bank[:], ALU.add)
        done_mm[t] = ctx.cnt["mm"]
        ctx.wait(nc.gpsimd, "ev")
        gs = DmaGroup(ctx, "st")
        for f in range(nf):
            gs.st(T["x1"][f * 128:(f + 1) * 128, t * TT:(t + 1) * TT],
                  x1t[:, f * TT:(f + 1) * TT])
        st_grp[t] = gs
    ctx.barrier()
    ctx.pop_frame()



def emit_l1a_qkv(ctx, T):
    nc = ctx.nc
    ctx.push_frame()
    nk = D // 128
    nf = 3072 // 128
    wb = load_weight_bf(ctx, "WqkvT", T["WqkvT"], D, 3072)
    rhs = [ctx.sbuf(f"qk_r{p}", [128, nk * TT], BF16) for p in range(2)]
    NS = 6
    tq = ctx.sbuf("qk_t", [128, TT], F32)
    ob = ctx.sbuf("qk_ob", [128, NS * TT], BF16)
    ps = [ctx.psum(f"qk_ps{q}", [128, TT]) for q in range(4)]
    done_mm = {}
    slot_st = {}
    gidx = 0
    for t in range(NTT):
        p = t % 2
        if t >= 2:
            ctx.wait(nc.sync, "mm", done_mm[t - 2])
        g = DmaGroup(ctx, "ld")
        for k in range(nk):
            g.ld(rhs[p][:, k * TT:(k + 1) * TT],
                 T["xn1b"][k * 128:(k + 1) * 128, t * TT:(t + 1) * TT])
        g.wait(nc.tensor)
        if t == 0:
            ctx.wait(nc.tensor, "ev")
        for f in range(nf):
            bank = ps[gidx % 4]
            ctx.wait(nc.tensor, "ev")
            ctx.wait(nc.tensor, "ac")
            last = None
            for k in range(nk):
                last = nc.tensor.matmul(bank[:], wb[:, k * 3072 + f * 128: k * 3072 + (f + 1) * 128],
                                        rhs[p][:, k * TT:(k + 1) * TT],
                                        start=(k == 0), stop=(k == nk - 1))
            ctx.inc(last, "mm")
            slot = gidx % NS
            if gidx >= NS:
                if f < 16:
                    slot_st[gidx - NS].wait(nc.scalar)
                slot_st[gidx - NS].wait(nc.vector)
            if f < 16:
                ctx.wait(nc.scalar, "mm")
                ctx.wait(nc.scalar, "ev")
                S(ctx, "activation", tq[:], bank[:], AF.Square, bias=0.0, scale=1.0)
                S(ctx, "activation", ob[:, slot * TT:(slot + 1) * TT], tq[:],
                  AF.Exp, bias=0.0, scale=-0.5)
                ctx.wait(nc.gpsimd, "ac")
            else:
                ctx.wait(nc.vector, "mm")
                V(ctx, "tensor_copy", ob[:, slot * TT:(slot + 1) * TT], bank[:])
                ctx.wait(nc.gpsimd, "ev")
            gs = DmaGroup(ctx, "st")
            dst = T["q1s"] if f < 8 else (T["k1s"] if f < 16 else T["v1s"])
            fr = f % 8
            gs.st(dst[fr * 128:(fr + 1) * 128, t * TT:(t + 1) * TT],
                  ob[:, slot * TT:(slot + 1) * TT])
            slot_st[gidx] = gs
            gidx += 1
        done_mm[t] = ctx.cnt["mm"]
    ctx.barrier()
    ctx.pop_frame()



def emit_l1a_core(ctx, T):
    nc = ctx.nc
    ctx.push_frame()
    qsl = [ctx.sbuf(f"la_q{p}", [64, SEQ], BF16) for p in range(2)]
    ksl = [ctx.sbuf(f"la_k{p}", [64, SEQ], BF16) for p in range(2)]
    vsl = [ctx.sbuf(f"la_v{p}", [64, SEQ], BF16) for p in range(2)]
    ktm = ctx.sbuf("la_ktm", [128, 16 * 64], BF16)
    vtm = ctx.sbuf("la_vtm", [128, 16 * 65], BF16)
    kvb = ctx.sbuf("la_kvb", [64, 65], BF16)
    dd = ctx.sbuf("la_dd", [128, 1], F32)
    otm = ctx.sbuf("la_otm", [128, 64], BF16)
    aob = [ctx.sbuf(f"la_ao{p}", [64, SEQ], BF16) for p in range(2)]
    pt = ctx.psum("la_pt", [128, 64], BF16)
    pkv = ctx.psum("la_pkv", [64, 65], F32)
    po = [ctx.psum(f"la_po{q}", [128, 65], F32) for q in range(2)]
    pt2 = [ctx.psum(f"la_pt2{q}", [64, 128], BF16) for q in range(2)]
    idn = ctx.R["ident"]
    done_mm = {}
    st_grp = {}
    u = 0
    for b in range(B_CORE):
        for h in range(16):
            p = u % 2
            if u >= 2:
                ctx.wait(nc.sync, "mm", done_mm[u - 2])
            g = DmaGroup(ctx, "ld")
            g.ld(qsl[p][:], T["q1s"][h * 64:(h + 1) * 64, b * SEQ:(b + 1) * SEQ])
            g.ld(ksl[p][:], T["k1s"][h * 64:(h + 1) * 64, b * SEQ:(b + 1) * SEQ])
            g.ld(vsl[p][:], T["v1s"][h * 64:(h + 1) * 64, b * SEQ:(b + 1) * SEQ])
            ctx.wait(nc.vector, "mm")
            V(ctx, "memset", vtm[:], 1.0)
            g.wait(nc.tensor)
            ctx.wait(nc.tensor, "ev")
            for c in range(16):
                ctx.inc(nc.tensor.transpose(pt[:], ksl[p][:, c * 128:(c + 1) * 128],
                                            idn[0:64, 0:64]), "mm")
                ctx.wait(nc.vector, "mm")
                V(ctx, "tensor_copy", ktm[:, c * 64:(c + 1) * 64], pt[:])
                ctx.wait(nc.tensor, "ev")
                ctx.inc(nc.tensor.transpose(pt[:], vsl[p][:, c * 128:(c + 1) * 128],
                                            idn[0:64, 0:64]), "mm")
                ctx.wait(nc.vector, "mm")
                V(ctx, "tensor_copy", vtm[:, c * 65:c * 65 + 64], pt[:])
                ctx.wait(nc.tensor, "ev")
            last = None
            for c in range(16):
                last = nc.tensor.matmul(pkv[:], ktm[:, c * 64:(c + 1) * 64],
                                        vtm[:, c * 65:(c + 1) * 65],
                                        start=(c == 0), stop=(c == 15))
            ctx.inc(last, "mm")
            ctx.wait(nc.vector, "mm")
            V(ctx, "tensor_copy", kvb[:], pkv[:])
            ctx.wait(nc.tensor, "ev")
            if u >= 2:
                st_grp[u - 2].wait(nc.vector)
            for c in range(16):
                bank = po[c % 2]
                if c >= 2:
                    ctx.wait(nc.tensor, "ev")
                ctx.inc(nc.tensor.matmul(bank[:], qsl[p][:, c * 128:(c + 1) * 128],
                                         kvb[:], start=True, stop=True), "mm")
                ctx.wait(nc.vector, "mm")
                V(ctx, "tensor_single_scalar", dd[:], bank[:, 64:65], 1e-6, ALU.add)
                V(ctx, "reciprocal", dd[:], dd[:])
                V(ctx, "tensor_scalar_mul", otm[:], bank[:, 0:64], dd[:])
                ctx.wait(nc.tensor, "ev")
                ctx.inc(nc.tensor.transpose(pt2[c % 2][:], otm[:], idn[:]), "mm")
                ctx.wait(nc.vector, "mm")
                V(ctx, "tensor_copy", aob[p][:, c * 128:(c + 1) * 128], pt2[c % 2][:])
            done_mm[u] = ctx.cnt["mm"]
            ctx.wait(nc.gpsimd, "ev")
            gs = DmaGroup(ctx, "st")
            gs.st(T["ao"][h * 64:(h + 1) * 64, b * SEQ:(b + 1) * SEQ], aob[p][:])
            st_grp[u] = gs
            u += 1
    ctx.barrier()
    ctx.pop_frame()



def emit_l1a_proj(ctx, T):
    nc = ctx.nc
    ctx.push_frame()
    nk = D // 128
    nf = D // 128
    wb = load_weight_bf(ctx, "WprojT", T["WprojT"], D, D)
    rhs = [ctx.sbuf(f"pj_r{p}", [128, nk * TT], BF16) for p in range(2)]
    xr = ctx.sbuf("pj_x", [128, 4 * TT], F32)
    NS = 6
    ob = ctx.sbuf("pj_ob", [128, NS * TT], F32)
    ps = [ctx.psum(f"pj_ps{q}", [128, TT]) for q in range(4)]
    done_mm = {}
    slot_st = {}
    x_ev = {}
    gidx = 0
    for t in range(NTT):
        p = t % 2
        if t >= 2:
            ctx.wait(nc.sync, "mm", done_mm[t - 2])
        g = DmaGroup(ctx, "ld")
        for k in range(nk):
            g.ld(rhs[p][:, k * TT:(k + 1) * TT],
                 T["ao"][k * 128:(k + 1) * 128, t * TT:(t + 1) * TT])
        g.wait(nc.tensor)
        if t == 0:
            ctx.wait(nc.tensor, "ev")
        for f in range(nf):
            bank = ps[gidx % 4]
            xslot = gidx % 4
            if gidx >= 4:
                ctx.wait(nc.sync, "ev", x_ev[gidx - 4])
            gx = DmaGroup(ctx, "ldC")
            gx.ld(xr[:, xslot * TT:(xslot + 1) * TT],
                  T["x1"][f * 128:(f + 1) * 128, t * TT:(t + 1) * TT])
            if gidx >= 4:
                ctx.wait(nc.tensor, "ev")
            last = None
            for k in range(nk):
                last = nc.tensor.matmul(bank[:], wb[:, k * D + f * 128: k * D + (f + 1) * 128],
                                        rhs[p][:, k * TT:(k + 1) * TT],
                                        start=(k == 0), stop=(k == nk - 1))
            ctx.inc(last, "mm")
            slot = gidx % NS
            ctx.wait(nc.vector, "mm")
            gx.wait(nc.vector)
            if gidx >= NS:
                slot_st[gidx - NS].wait(nc.vector)
            V(ctx, "scalar_tensor_tensor", ob[:, slot * TT:(slot + 1) * TT], bank[:],
              ctx.R["bproj128"][:, f:f + 1], xr[:, xslot * TT:(xslot + 1) * TT],
              ALU.add, ALU.add)
            x_ev[gidx] = ctx.cnt["ev"]
            ctx.wait(nc.gpsimd, "ev")
            gs = DmaGroup(ctx, "st")
            gs.st(T["p2"][f * 128:(f + 1) * 128, t * TT:(t + 1) * TT],
                  ob[:, slot * TT:(slot + 1) * TT])
            slot_st[gidx] = gs
            gidx += 1
        done_mm[t] = ctx.cnt["mm"]
    ctx.barrier()
    ctx.pop_frame()



def _ucol(s, t0):
    return s * (SEQ + 1) + 1 + t0


def load_shifted_u(ctx, g, ubuf, dst_f32, dst_bf, t):
    nc = ctx.nc
    s, t0 = t // TPS, (t % TPS) * TT
    for k in range(8):
        col = _ucol(s, t0) if k < 4 else _ucol(s, t0) - 1
        g.ld(dst_f32[:, k * TT:(k + 1) * TT],
             ubuf[k * 128:(k + 1) * 128, col:col + TT])
    g.wait(nc.vector)
    for k in range(8):
        V(ctx, "tensor_copy", dst_bf[:, k * TT:(k + 1) * TT],
          dst_f32[:, k * TT:(k + 1) * TT])


def emit_f_qkv(ctx, T, i, ubuf):
    nc = ctx.nc
    ctx.push_frame()
    nm = f"fq{i}_"
    nk = 8
    stg = [ctx.sbuf(nm + "stg0", [128, 2048], F32), ctx.sbuf(nm + "stg1", [128, 2048], F32)]
    wqk = load_weight_bf(ctx, nm + "wqk", T["Wqkv6"][i], D, 2048, colmap=(0, 2048), stg=stg)
    wv = load_weight_bf(ctx, nm + "wv", T["Wqkv6"][i], D, 1024, colmap=(2048, 1024), stg=stg)
    uf = [ctx.sbuf(nm + f"uf{p}", [128, 8 * TT], F32) for p in range(2)]
    ub = [ctx.sbuf(nm + f"ub{p}", [128, 8 * TT], BF16) for p in range(2)]
    sqt = ctx.sbuf(nm + "sqt", [128, TT], BF16)
    qf = ctx.sbuf(nm + "qf", [128, TT], F32)
    sS = ctx.sbuf(nm + "sS", [2, TT], F32)
    qn = ctx.sbuf(nm + "qn", [128, TT], BF16)
    rot = ctx.sbuf(nm + "rot", [128, TT], BF16)
    tt1 = ctx.sbuf(nm + "tt1", [128, TT], BF16)
    NS = 4
    qr = ctx.sbuf(nm + "qr", [128, NS * TT], BF16)
    vo = ctx.sbuf(nm + "vo", [128, NS * TT], BF16)
    ps = [ctx.psum(nm + f"ps{q}", [128, TT]) for q in range(2)]
    psS = ctx.psum(nm + "psS", [2, TT])
    psB = ctx.psum(nm + "psB", [128, TT])
    psV = [ctx.psum(nm + f"psV{q}", [128, TT]) for q in range(2)]
    done_mm = {}
    slot_st = {}
    vslot_st = {}
    gidx = 0
    vgidx = 0
    for t in range(NTT):
        p = t % 2
        s, t0 = t // TPS, (t % TPS) * TT
        if t >= 2:
            ctx.wait(nc.sync, "mm", done_mm[t - 2])
            ctx.wait(nc.sync, "ev", done_mm[(t - 2, "ev")])
        g = DmaGroup(ctx, "ld")
        load_shifted_u(ctx, g, ubuf, uf[p], ub[p], t)
        ctx.wait(nc.tensor, "ev")
        if t == 0:
            pass
        for f in range(16):
            isq = f < 8
            fp = f % 8
            bank = ps[gidx % 2]
            cs = ctx.R["csq" if isq else "csk"][:, t0:t0 + TT]
            ss = ctx.R["ssq" if isq else "ssk"][:, t0:t0 + TT]
            sc6 = ctx.R["qs6" if isq else "ks6"][:, i:i + 1]
            ctx.wait(nc.tensor, "ac")
            last = None
            for k in range(nk):
                last = nc.tensor.matmul(bank[:], wqk[:, k * 2048 + f * 128:k * 2048 + (f + 1) * 128],
                                        ub[p][:, k * TT:(k + 1) * TT],
                                        start=(k == 0), stop=(k == nk - 1))
            ctx.inc(last, "mm")
            ctx.wait(nc.scalar, "mm")
            ctx.wait(nc.scalar, "ev")
            S(ctx, "activation", sqt[:], bank[:], AF.Square, bias=0.0, scale=1.0)
            S(ctx, "activation", qf[:], bank[:], AF.Copy, bias=0.0, scale=1.0)
            ctx.wait(nc.tensor, "ac")
            ctx.inc(nc.tensor.matmul(psS[:], ctx.R["ones2"][:], sqt[:],
                                     start=True, stop=True), "mm")
            ctx.wait(nc.vector, "mm")
            V(ctx, "tensor_single_scalar", sS[:], psS[:], 1e-24, ALU.max)
            ctx.wait(nc.scalar, "ev")
            S(ctx, "activation", sS[:], sS[:], AF.Ln, bias=0.0, scale=1.0)
            S(ctx, "activation", sS[:], sS[:], AF.Exp, bias=0.0, scale=-0.5)
            ctx.wait(nc.tensor, "ac")
            ctx.wait(nc.tensor, "ev")
            ctx.inc(nc.tensor.matmul(psB[:], ctx.R["sel2f"][:], sS[:],
                                     start=True, stop=True), "mm")
            ctx.wait(nc.vector, "mm")
            ctx.wait(nc.vector, "ac")
            V(ctx, "scalar_tensor_tensor", qn[:], qf[:], sc6, psB[:], ALU.mult, ALU.mult)
            V(ctx, "tensor_scalar_mul", rot[0:32, :], qn[32:64, :], -1.0)
            V(ctx, "tensor_copy", rot[32:64, :], qn[0:32, :])
            V(ctx, "tensor_scalar_mul", rot[64:96, :], qn[96:128, :], -1.0)
            V(ctx, "tensor_copy", rot[96:128, :], qn[64:96, :])
            slot = gidx % NS
            if gidx >= NS:
                slot_st[gidx - NS].wait(nc.vector)
            V(ctx, "tensor_tensor", tt1[:], qn[:], cs, ALU.mult)
            V(ctx, "tensor_tensor", qr[:, slot * TT:(slot + 1) * TT], rot[:], ss, ALU.mult)
            V(ctx, "tensor_tensor", qr[:, slot * TT:(slot + 1) * TT], tt1[:],
              qr[:, slot * TT:(slot + 1) * TT], ALU.add)
            ctx.wait(nc.gpsimd, "ev")
            gs = DmaGroup(ctx, "st")
            if isq:
                gs.st(T["qscr"][fp * 128:(fp + 1) * 128, s * SEQ + t0:s * SEQ + t0 + TT],
                      qr[:, slot * TT:(slot + 1) * TT])
            else:
                gs.st(T["kscr"][fp * 128:(fp + 1) * 128,
                                s * (WIN + SEQ) + WIN + t0:s * (WIN + SEQ) + WIN + t0 + TT],
                      qr[:, slot * TT:(slot + 1) * TT])
            slot_st[gidx] = gs
            gidx += 1
        for c4 in range(4):
            for c2 in range(2):
                bank = psV[vgidx % 2]
                ctx.wait(nc.tensor, "ev")
                last = None
                for k in range(nk):
                    last = nc.tensor.matmul(bank[:], ub[p][:, k * TT + c4 * 128:k * TT + (c4 + 1) * 128],
                                            wv[:, k * 1024 + c2 * TT:k * 1024 + (c2 + 1) * TT],
                                            start=(k == 0), stop=(k == nk - 1))
                ctx.inc(last, "mm")
                vslot = vgidx % NS
                ctx.wait(nc.vector, "mm")
                if vgidx >= NS:
                    vslot_st[vgidx - NS].wait(nc.vector)
                V(ctx, "tensor_copy", vo[:, vslot * TT:(vslot + 1) * TT], bank[:])
                ctx.wait(nc.gpsimd, "ev")
                gs = DmaGroup(ctx, "stB")
                j = (t % TPS) * 4 + c4
                for hh in range(8):
                    h = c2 * 8 + hh
                    base = (s * 16 + h) * 18 * 128
                    gs.st(T["vscr"][base + (j + 2) * 128:base + (j + 3) * 128, :],
                          vo[:, vslot * TT + hh * 64:vslot * TT + (hh + 1) * 64])
                vslot_st[vgidx] = gs
                vgidx += 1
        done_mm[t] = ctx.cnt["mm"]
        done_mm[(t, "ev")] = ctx.cnt["ev"]
    ctx.barrier()
    ctx.pop_frame()


def emit_f_attn(ctx, T, i):
    nc = ctx.nc
    ctx.push_frame()
    nm = f"fa{i}_"
    qsl = [ctx.sbuf(nm + f"q{p}", [64, SEQ], BF16) for p in range(2)]
    ksl = [ctx.sbuf(nm + f"k{p}", [64, WIN + SEQ], BF16) for p in range(2)]
    vsl = [ctx.sbuf(nm + f"v{p}", [128, 18 * 64], BF16) for p in range(2)]
    nmx = ctx.sbuf(nm + "nm", [128, 2], F32)
    ea = ctx.sbuf(nm + "ea", [128, 384], BF16)
    eaT = ctx.sbuf(nm + "eaT", [128, 384], BF16)
    aob = [ctx.sbuf(nm + f"ao{p}", [64, SEQ], BF16) for p in range(2)]
    psS = [ctx.psum(nm + f"psS{q}", [128, 384], F32) for q in range(2)]
    psT = ctx.psum(nm + "psT", [128, 384], BF16)
    psO = [ctx.psum(nm + f"psO{q}", [64, 128], F32) for q in range(2)]
    idn = ctx.R["ident"]
    done_mm = {}
    st_grp = {}
    u = 0
    for b in range(B_CORE):
        for h in range(16):
            p = u % 2
            if u >= 2:
                ctx.wait(nc.sync, "mm", done_mm[u - 2])
                ctx.wait(nc.sync, "ev", done_mm[(u - 2, "ev")])
            g = DmaGroup(ctx, "ld")
            g.ld(qsl[p][:], T["qscr"][(h // 2) * 128 + (h % 2) * 64:(h // 2) * 128 + (h % 2) * 64 + 64,
                                      b * SEQ:(b + 1) * SEQ])
            g.ld(ksl[p][:], T["kscr"][(h // 2) * 128 + (h % 2) * 64:(h // 2) * 128 + (h % 2) * 64 + 64,
                                      b * (WIN + SEQ):(b + 1) * (WIN + SEQ)])
            base = (b * 16 + h) * 18 * 128
            g.ld(vsl[p][:].rearrange("q (j e) -> q j e", j=18),
                 T["vscr"][base:base + 18 * 128, :].rearrange("(j q) e -> q j e", q=128))
            g.wait(nc.tensor)
            if u >= 2:
                st_grp[u - 2].wait(nc.vector)
            for j in range(16):
                bS = psS[j % 2]
                bO = psO[j % 2]
                variant = 2 if j == 0 else (1 if j == 1 else 0)
                ctx.wait(nc.tensor, "ac")
                nc.tensor.matmul(bS[:], qsl[p][:, j * 128:(j + 1) * 128],
                                 ksl[p][:, j * 128:j * 128 + 384],
                                 start=True, stop=False)
                ctx.inc(nc.tensor.matmul(bS[:], idn[:],
                                         ctx.R["masks"][:, variant * 384:(variant + 1) * 384],
                                         start=False, stop=True), "mm")
                ctx.wait(nc.vector, "mm")
                V(ctx, "tensor_reduce", nmx[:, 0:1], bS[:], axis=AX.X, op=ALU.max,
                  negate=True)
                ctx.wait(nc.scalar, "ev")
                ctx.wait(nc.scalar, "mm")
                S(ctx, "activation", ea[:], bS[:], AF.Exp, bias=nmx[:, 0:1], scale=1.0,
                  accum_out=nmx[:, 1:2])
                ctx.wait(nc.vector, "ac")
                V(ctx, "reciprocal", nmx[:, 1:2], nmx[:, 1:2])
                V(ctx, "tensor_scalar_mul", ea[:], ea[:], nmx[:, 1:2])
                ctx.wait(nc.tensor, "ev")
                last = None
                for c in range(3):
                    last = nc.tensor.transpose(psT[:, c * 128:(c + 1) * 128],
                                               ea[:, c * 128:(c + 1) * 128], idn[:])
                ctx.inc(last, "mm")
                ctx.wait(nc.vector, "mm")
                V(ctx, "tensor_copy", eaT[:], psT[:])
                ctx.wait(nc.tensor, "ev")
                last = None
                for c in range(3):
                    last = nc.tensor.matmul(bO[:], vsl[p][:, (j + c) * 64:(j + c + 1) * 64],
                                            eaT[:, c * 128:(c + 1) * 128],
                                            start=(c == 0), stop=(c == 2))
                ctx.inc(last, "mm")
                ctx.wait(nc.vector, "mm")
                V(ctx, "tensor_copy", aob[p][:, j * 128:(j + 1) * 128], bO[:])
            done_mm[u] = ctx.cnt["mm"]
            done_mm[(u, "ev")] = ctx.cnt["ev"]
            ctx.wait(nc.gpsimd, "ev")
            gs = DmaGroup(ctx, "st")
            gs.st(T["ao"][h * 64:(h + 1) * 64, b * SEQ:(b + 1) * SEQ], aob[p][:])
            st_grp[u] = gs
            u += 1
    ctx.barrier()
    ctx.pop_frame()


def emit_f_wo(ctx, T, i, ubuf_in, ubuf_out):
    nc = ctx.nc
    ctx.push_frame()
    nm = f"fw{i}_"
    nk = 8
    nf = 8
    wb = load_weight_bf(ctx, nm + "wo", T["Wo6"][i], D, D)
    rhs = [ctx.sbuf(nm + f"r{p}", [128, nk * TT], BF16) for p in range(2)]
    xr = ctx.sbuf(nm + "x", [128, 4 * TT], F32)
    NS = 6
    ob = ctx.sbuf(nm + "ob", [128, NS * TT], F32)
    ps = [ctx.psum(nm + f"ps{q}", [128, TT]) for q in range(4)]
    done_mm = {}
    slot_st = {}
    x_ev = {}
    gidx = 0
    for t in range(NTT):
        p = t % 2
        s, t0 = t // TPS, (t % TPS) * TT
        if t >= 2:
            ctx.wait(nc.sync, "mm", done_mm[t - 2])
        g = DmaGroup(ctx, "ld")
        for k in range(nk):
            g.ld(rhs[p][:, k * TT:(k + 1) * TT],
                 T["ao"][k * 128:(k + 1) * 128, t * TT:(t + 1) * TT])
        g.wait(nc.tensor)
        if t == 0:
            ctx.wait(nc.tensor, "ev")
        for f in range(nf):
            bank = ps[gidx % 4]
            xslot = gidx % 4
            if gidx >= 4:
                ctx.wait(nc.sync, "ev", x_ev[gidx - 4])
            gx = DmaGroup(ctx, "ldC")
            gx.ld(xr[:, xslot * TT:(xslot + 1) * TT],
                  ubuf_in[f * 128:(f + 1) * 128, _ucol(s, t0):_ucol(s, t0) + TT])
            if gidx >= 4:
                ctx.wait(nc.tensor, "ev")
            last = None
            for k in range(nk):
                last = nc.tensor.matmul(bank[:], wb[:, k * D + f * 128: k * D + (f + 1) * 128],
                                        rhs[p][:, k * TT:(k + 1) * TT],
                                        start=(k == 0), stop=(k == nk - 1))
            ctx.inc(last, "mm")
            slot = gidx % NS
            ctx.wait(nc.vector, "mm")
            gx.wait(nc.vector)
            if gidx >= NS:
                slot_st[gidx - NS].wait(nc.vector)
            V(ctx, "tensor_tensor", ob[:, slot * TT:(slot + 1) * TT], bank[:],
              xr[:, xslot * TT:(xslot + 1) * TT], ALU.add)
            x_ev[gidx] = ctx.cnt["ev"]
            ctx.wait(nc.gpsimd, "ev")
            gs = DmaGroup(ctx, "st")
            gs.st(ubuf_out[f * 128:(f + 1) * 128, _ucol(s, t0):_ucol(s, t0) + TT],
                  ob[:, slot * TT:(slot + 1) * TT])
            slot_st[gidx] = gs
            gidx += 1
        done_mm[t] = ctx.cnt["mm"]
    ctx.barrier()
    ctx.pop_frame()


def emit_f_ln(ctx, T, i, ubuf):
    nc = ctx.nc
    ctx.push_frame()
    nm = f"fl{i}_"
    nk = 8
    xb = [ctx.sbuf(nm + f"x{p}", [128, nk * TT], F32) for p in range(2)]
    sq = [ctx.sbuf(nm + f"q{p}", [128, nk * TT], BF16) for p in range(2)]
    ob = [ctx.sbuf(nm + f"ob{p}", [128, nk * TT], BF16) for p in range(2)]
    sc = ctx.sbuf(nm + "sc", [1, 8 * TT], F32)
    rb = [ctx.sbuf(nm + f"rb{p}", [128, TT], F32) for p in range(2)]
    ot = [ctx.sbuf(nm + f"o{p}", [128, TT], F32) for p in range(2)]
    psA = [ctx.psum(nm + "psA", [1, TT])] * 2
    psB = [ctx.psum(nm + "psB", [1, TT])] * 2
    psM = [ctx.psum(nm + "psM", [128, TT])] * 2
    psR = [ctx.psum(nm + "psR", [128, TT])] * 2
    done_ev = {}
    done_sq = {}
    st_grp = {}
    for t in range(NTT):
        p = t % 2
        if t >= 2:
            ctx.wait(nc.sync, "ev", done_ev[t - 2])
            ctx.wait(nc.sync, "ac", done_sq[t - 2])
        g = DmaGroup(ctx, "ld")
        s, t0 = t // TPS, (t % TPS) * TT
        for k in range(nk):
            col = _ucol(s, t0) if k < 4 else _ucol(s, t0) - 1
            g.ld(xb[p][:, k * TT:(k + 1) * TT],
                 ubuf[k * 128:(k + 1) * 128, col:col + TT])
        g.wait(nc.scalar)
        for k in range(nk):
            S(ctx, "activation", sq[p][:, k * TT:(k + 1) * TT],
              xb[p][:, k * TT:(k + 1) * TT], AF.Square, bias=0.0, scale=1.0)
        done_sq[t] = ctx.cnt["ac"]
        ctx.wait(nc.tensor, "ev")
        ctx.wait(nc.tensor, "ac")
        for k in range(nk):
            nc.tensor.matmul(psA[p][:], ctx.R["ones128f"][:], xb[p][:, k * TT:(k + 1) * TT],
                             start=(k == 0), stop=(k == nk - 1))
        last = None
        for k in range(nk):
            last = nc.tensor.matmul(psB[p][:], ctx.R["ones128b"][:], sq[p][:, k * TT:(k + 1) * TT],
                                    start=(k == 0), stop=(k == nk - 1))
        ctx.inc(last, "mm")
        mu = sc[:, (4 * p + 0) * TT:(4 * p + 1) * TT]
        vv = sc[:, (4 * p + 1) * TT:(4 * p + 2) * TT]
        ms = sc[:, (4 * p + 2) * TT:(4 * p + 3) * TT]
        rs = sc[:, (4 * p + 3) * TT:(4 * p + 4) * TT]
        ctx.wait(nc.scalar, "mm")
        S(ctx, "activation", mu, psA[p][:], AF.Copy, bias=0.0, scale=1.0 / D)
        S(ctx, "activation", vv, psB[p][:], AF.Copy, bias=0.0, scale=1.0 / D)
        S(ctx, "activation", ms, mu, AF.Square, bias=0.0, scale=1.0)
        ctx.wait(nc.vector, "ac")
        V(ctx, "tensor_tensor", vv, vv, ms, ALU.subtract)
        ctx.wait(nc.scalar, "ev")
        S(ctx, "activation", vv, vv, AF.Ln, bias=cbias(ctx, 1e-5, 1), scale=1.0)
        S(ctx, "activation", rs, vv, AF.Exp, bias=0.0, scale=-0.5)
        ctx.wait(nc.tensor, "ac")
        bcast_cols(ctx, psM[p][:], mu)
        ctx.inc(bcast_cols(ctx, psR[p][:], rs), "mm")
        ctx.wait(nc.vector, "mm")
        if t >= 2:
            st_grp[t - 2].wait(nc.vector)
        V(ctx, "tensor_copy", rb[p][:], psR[p][:])
        for k in range(nk):
            V(ctx, "tensor_tensor", ot[p][:], xb[p][:, k * TT:(k + 1) * TT],
              psM[p][:], ALU.subtract)
            V(ctx, "tensor_tensor", ot[p][:], ot[p][:], rb[p][:], ALU.mult)
            V(ctx, "tensor_scalar", ob[p][:, k * TT:(k + 1) * TT], ot[p][:],
              ctx.R["flng128"][:, i * 8 + k:i * 8 + k + 1],
              ctx.R["flnb128"][:, i * 8 + k:i * 8 + k + 1], ALU.mult, ALU.add)
        done_ev[t] = ctx.cnt["ev"]
        ctx.wait(nc.gpsimd, "ev")
        gs = DmaGroup(ctx, "st")
        for k in range(nk):
            gs.st(T["xnf"][k * 128:(k + 1) * 128, t * TT:(t + 1) * TT],
                  ob[p][:, k * TT:(k + 1) * TT])
        st_grp[t] = gs
    ctx.barrier()
    ctx.pop_frame()


def emit_f_w1(ctx, T, i):
    nc = ctx.nc
    ctx.push_frame()
    nm = f"fm{i}_"
    nk = 8
    nfc = FFN_IP // 128
    wb = load_weight_bf(ctx, nm + "w1", T["W1p"][i], D, 2 * FFN_IP)
    rhs = [ctx.sbuf(nm + f"r{p}", [128, nk * TT], BF16) for p in range(2)]
    gel = ctx.sbuf(nm + "gel", [128, TT], F32)
    NS = 4
    ob = ctx.sbuf(nm + "ob", [128, NS * TT], BF16)
    ps = [ctx.psum(nm + f"ps{q}", [128, TT]) for q in range(4)]
    done_mm = {}
    slot_st = {}
    gidx = 0
    W = 2 * FFN_IP
    for t in range(NTT):
        p = t % 2
        if t >= 2:
            ctx.wait(nc.sync, "mm", done_mm[t - 2])
        g = DmaGroup(ctx, "ld")
        for k in range(nk):
            g.ld(rhs[p][:, k * TT:(k + 1) * TT],
                 T["xnf"][k * 128:(k + 1) * 128, t * TT:(t + 1) * TT])
        g.wait(nc.tensor)
        if t == 0:
            ctx.wait(nc.tensor, "ev")
        for fc in range(nfc):
            bg = ps[(gidx % 2) * 2]
            ba = ps[(gidx % 2) * 2 + 1]
            ctx.wait(nc.tensor, "ac")
            ctx.wait(nc.tensor, "ev")
            for k in range(nk):
                nc.tensor.matmul(bg[:], wb[:, k * W + (FFN_IP + fc * 128):k * W + (FFN_IP + (fc + 1) * 128)],
                                 rhs[p][:, k * TT:(k + 1) * TT],
                                 start=(k == 0), stop=(k == nk - 1))
            last = None
            for k in range(nk):
                last = nc.tensor.matmul(ba[:], wb[:, k * W + fc * 128:k * W + (fc + 1) * 128],
                                        rhs[p][:, k * TT:(k + 1) * TT],
                                        start=(k == 0), stop=(k == nk - 1))
            ctx.inc(last, "mm")
            ctx.wait(nc.scalar, "mm")
            ctx.wait(nc.scalar, "ev")
            S(ctx, "activation", gel[:], bg[:], AF.Gelu,
              bias=ctx.R["b1p128"][:, i * 44 + 22 + fc:i * 44 + 22 + fc + 1], scale=1.0)
            slot = gidx % NS
            ctx.wait(nc.vector, "ac")
            ctx.wait(nc.vector, "mm")
            if gidx >= NS:
                slot_st[gidx - NS].wait(nc.vector)
            V(ctx, "scalar_tensor_tensor", ob[:, slot * TT:(slot + 1) * TT], ba[:],
              ctx.R["b1p128"][:, i * 44 + fc:i * 44 + fc + 1], gel[:],
              ALU.add, ALU.mult)
            ctx.wait(nc.gpsimd, "ev")
            gs = DmaGroup(ctx, "st")
            gs.st(T["hgt"][fc * 128:(fc + 1) * 128, t * TT:(t + 1) * TT],
                  ob[:, slot * TT:(slot + 1) * TT])
            slot_st[gidx] = gs
            gidx += 1
        done_mm[t] = ctx.cnt["mm"]
    ctx.barrier()
    ctx.pop_frame()


def emit_f_w2(ctx, T, i, ubuf_in, ubuf_out):
    nc = ctx.nc
    ctx.push_frame()
    nm = f"fz{i}_"
    nk = FFN_IP // 128
    nf = 8
    wb = load_weight_bf(ctx, nm + "w2", T["W2p"][i], FFN_IP, D)
    rhs = [ctx.sbuf(nm + f"r{p}", [128, nk * TT], BF16) for p in range(2)]
    xr = ctx.sbuf(nm + "x", [128, 4 * TT], F32)
    NS = 6
    ob = ctx.sbuf(nm + "ob", [128, NS * TT], F32)
    ps = [ctx.psum(nm + f"ps{q}", [128, TT]) for q in range(4)]
    done_mm = {}
    slot_st = {}
    x_ev = {}
    gidx = 0
    for t in range(NTT):
        p = t % 2
        s, t0 = t // TPS, (t % TPS) * TT
        if t >= 2:
            ctx.wait(nc.sync, "mm", done_mm[t - 2])
        g = DmaGroup(ctx, "ld")
        for k in range(nk):
            g.ld(rhs[p][:, k * TT:(k + 1) * TT],
                 T["hgt"][k * 128:(k + 1) * 128, t * TT:(t + 1) * TT])
        g.wait(nc.tensor)
        if t == 0:
            ctx.wait(nc.tensor, "ev")
        for f in range(nf):
            bank = ps[gidx % 4]
            xslot = gidx % 4
            if gidx >= 4:
                ctx.wait(nc.sync, "ev", x_ev[gidx - 4])
            gx = DmaGroup(ctx, "ldC")
            gx.ld(xr[:, xslot * TT:(xslot + 1) * TT],
                  ubuf_in[f * 128:(f + 1) * 128, _ucol(s, t0):_ucol(s, t0) + TT])
            if gidx >= 4:
                ctx.wait(nc.tensor, "ev")
            last = None
            for k in range(nk):
                last = nc.tensor.matmul(bank[:], wb[:, k * D + f * 128: k * D + (f + 1) * 128],
                                        rhs[p][:, k * TT:(k + 1) * TT],
                                        start=(k == 0), stop=(k == nk - 1))
            ctx.inc(last, "mm")
            slot = gidx % NS
            ctx.wait(nc.vector, "mm")
            gx.wait(nc.vector)
            if gidx >= NS:
                slot_st[gidx - NS].wait(nc.vector)
            V(ctx, "scalar_tensor_tensor", ob[:, slot * TT:(slot + 1) * TT], bank[:],
              ctx.R["b2128"][:, i * 8 + f:i * 8 + f + 1], xr[:, xslot * TT:(xslot + 1) * TT],
              ALU.add, ALU.add)
            x_ev[gidx] = ctx.cnt["ev"]
            ctx.wait(nc.gpsimd, "ev")
            gs = DmaGroup(ctx, "st")
            gs.st(ubuf_out[f * 128:(f + 1) * 128, _ucol(s, t0):_ucol(s, t0) + TT],
                  ob[:, slot * TT:(slot + 1) * TT])
            slot_st[gidx] = gs
            gidx += 1
        done_mm[t] = ctx.cnt["mm"]
    ctx.barrier()
    ctx.pop_frame()



def emit_final(ctx, T, ubuf):
    nc = ctx.nc
    ctx.push_frame()
    nf = 8
    a = [ctx.sbuf(f"fn_a{p}", [128, nf * TT], F32) for p in range(2)]
    b = [ctx.sbuf(f"fn_b{p}", [128, nf * TT], F32) for p in range(2)]
    done_ev = {}
    st_grp = {}
    for t in range(NTT):
        p = t % 2
        s, t0 = t // TPS, (t % TPS) * TT
        if t >= 2:
            ctx.wait(nc.sync, "ev", done_ev[t - 2])
            st_grp[t - 2].wait(nc.sync)
        g = DmaGroup(ctx, "ld")
        for f in range(nf):
            g.ld(a[p][:, f * TT:(f + 1) * TT],
                 T["p2"][f * 128:(f + 1) * 128, t * TT:(t + 1) * TT])
            g.ld(b[p][:, f * TT:(f + 1) * TT],
                 ubuf[f * 128:(f + 1) * 128, _ucol(s, t0):_ucol(s, t0) + TT])
        g.wait(nc.vector)
        for f in range(nf):
            V(ctx, "tensor_tensor", a[p][:, f * TT:(f + 1) * TT],
              a[p][:, f * TT:(f + 1) * TT], b[p][:, f * TT:(f + 1) * TT], ALU.add)
        done_ev[t] = ctx.cnt["ev"]
        ctx.wait(nc.gpsimd, "ev")
        gs = DmaGroup(ctx, "st")
        for f in range(nf):
            gs.st(T["out_fm"][f * 128:(f + 1) * 128, t * TT:(t + 1) * TT],
                  a[p][:, f * TT:(f + 1) * TT])
        st_grp[t] = gs
    ctx.barrier()
    ctx.pop_frame()



PHASES = ["p0", "p1", "p2", "p3", "p4", "p5", "p6", "p7", "p8", "p9",
          "f0", "f1", "f2", "f3", "f4", "f5", "p11"]

def build_program(upto=None, dbg=()):
    nc = bass.Bass("TRN2", target_bir_lowering=False)
    T = declare_tensors(nc, dbg)
    ctx = Ctx(nc)
    def stop(name):
        return upto is not None and PHASES.index(upto) < PHASES.index(name)
    emit_consts(ctx, T)
    if not stop("p0"):
        emit_outer_ln(ctx, T, T["x_fm"], dst_bf=T["xn0b"], dst_f32=T["xn0f"], pfx="ln0")
    if not stop("p1"):
        emit_l0_inproj(ctx, T)
    if not stop("p2"):
        emit_l0_conv(ctx, T)
    if not stop("p3"):
        emit_l0_dt(ctx, T)
    if not stop("p4"):
        emit_l0_scan(ctx, T)
    if not stop("p5"):
        emit_l0_outproj(ctx, T)
    if not stop("p6"):
        emit_moe_gate(ctx, T)
    if not stop("p7"):
        emit_moe_expert(ctx, T, -1)
        for e in range(8):
            emit_moe_expert(ctx, T, e)
    if not stop("p8"):
        emit_outer_ln(ctx, T, T["x1"], dst_bf=T["xn1b"], dst_f32=T["uA"],
                      dst_zerocol=True, pfx="ln1")
    if not stop("p9"):
        emit_l1a_qkv(ctx, T)
        emit_l1a_core(ctx, T)
        emit_l1a_proj(ctx, T)
    bufs = [T["uA"], T["uB"]]
    for i in range(6):
        if stop(f"f{i}"):
            break
        emit_f_qkv(ctx, T, i, bufs[0])
        emit_f_attn(ctx, T, i)
        emit_f_wo(ctx, T, i, bufs[0], bufs[1])
        emit_f_ln(ctx, T, i, bufs[1])
        emit_f_w1(ctx, T, i)
        emit_f_w2(ctx, T, i, bufs[1], bufs[0])
    if not stop("p11"):
        emit_final(ctx, T, bufs[0])
    ctx.close()
    return nc

def kernel(**inputs):
    sh, percore = prep_inputs(inputs)
    nc = build_program()
    in_maps = [dict(sh, **pc) for pc in percore]
    res = run_bass_kernel_spmd(nc, in_maps, core_ids=list(range(NCORES)))
    out = np.empty((32, 2048, 1024), np.float32)
    for c in range(NCORES):
        out[c * B_CORE:(c + 1) * B_CORE] = \
            res.results[c]["out_fm"].T.reshape(B_CORE, SEQ, D)
    return out


# version 03 (final)
# speedup vs baseline: 1.1221x; 1.1221x over previous
import numpy as np
import ml_dtypes
from contextlib import ExitStack
from collections import defaultdict

import concourse.bass as bass
import concourse.mybir as mybir
from concourse.bass_utils import run_bass_kernel_spmd

F32 = mybir.dt.float32
BF16 = mybir.dt.bfloat16
AF = mybir.ActivationFunctionType
ALU = mybir.AluOpType
AX = mybir.AxisListType
BF = ml_dtypes.bfloat16

NCORES = 8
B_CORE = 4
SEQ = 2048
TOK = B_CORE * SEQ
D = 1024
DI = 2048
TT = 512
NTT = TOK // TT
TPS = SEQ // TT
NEG = np.float32(-1e30)
FFN_I = 2730
FFN_IP = 2816
WIN = 256
EPS = np.float32(1e-5)


def _t(a):
    return np.ascontiguousarray(np.asarray(a).transpose())

def _bias128(b):
    return np.ascontiguousarray(np.asarray(b).reshape(-1, 128).T)

def _rotary_tables():
    n, DH, W = SEQ, 64, WIN
    half = np.arange(0, DH, 2, dtype=np.float32)
    inv_freq = 1.0 / 10000.0 ** (half / DH)
    t = np.arange(n, dtype=np.float32)
    freqs = t[:, None] * inv_freq[None, :]
    freqs = np.concatenate([freqs, freqs], -1)
    base = (half + 0.4 * DH) / (1.4 * DH)
    power = (t - n // 2) / (W // 2)
    scale = base[None, :] ** power[:, None]
    scale = np.concatenate([scale, scale], -1).astype(np.float32)
    def fm2(x):
        x = np.ascontiguousarray(x.T.astype(np.float32))
        return np.ascontiguousarray(np.concatenate([x, x], 0)).astype(BF)
    return (fm2(np.cos(freqs) * scale * 8.0), fm2(np.sin(freqs) * scale * 8.0),
            fm2(np.cos(freqs) / scale), fm2(np.sin(freqs) / scale))

def _band_masks():
    dm = np.arange(128)[:, None]
    f = np.arange(384)[None, :]
    band = (f >= dm) & (f <= dm + 256)
    m = np.zeros((3, 128, 384), np.float32)
    m[0] = np.where(band, 0.0, NEG)
    m[1] = np.where(band & (f >= 128), 0.0, NEG)
    m[2] = np.where(band & (f >= 256), 0.0, NEG)
    return np.ascontiguousarray(m.astype(BF))

def prep_inputs(p):
    sh = {}
    sh["w_in"] = np.ascontiguousarray(p["L0_attn_w_in"], dtype=np.float32)
    sh["b_in128"] = _bias128(p["L0_attn_b_in"])
    sh["convw128"] = np.ascontiguousarray(
        np.asarray(p["L0_attn_conv_w"])[:, 0, :].reshape(16, 128, 4)
        .transpose(1, 0, 2).reshape(128, 64)).astype(np.float32)
    sh["convb128"] = _bias128(p["L0_attn_conv_b"])
    sh["w_dt"] = np.ascontiguousarray(p["L0_attn_w_dt"], dtype=np.float32)
    sh["b_dt128"] = _bias128(p["L0_attn_b_dt"])
    sh["dskip128"] = _bias128(p["L0_attn_D_skip"])
    sh["w_out"] = np.ascontiguousarray(p["L0_attn_w_out"], dtype=np.float32)
    sh["b_out128"] = _bias128(p["L0_attn_b_out"])
    sh["lng128"] = _bias128(p["L0_attn_ln_g"])
    sh["lnb128"] = _bias128(p["L0_attn_ln_b"])
    sh["gate_wT"] = _t(p["L0_ffn_gate_w"])
    sh["gate_b8"] = np.ascontiguousarray(np.asarray(p["L0_ffn_gate_bias"],
                                                    dtype=np.float32).reshape(1, 8))
    sh["w1T"] = np.ascontiguousarray(np.asarray(p["L0_ffn_w1"]).transpose(0, 2, 1))
    sh["w3T"] = np.ascontiguousarray(np.asarray(p["L0_ffn_w3"]).transpose(0, 2, 1))
    sh["w2T"] = np.ascontiguousarray(np.asarray(p["L0_ffn_w2"]).transpose(0, 2, 1))
    sh["sw1T"] = _t(p["L0_ffn_sw1"])
    sh["sw3T"] = _t(p["L0_ffn_sw3"])
    sh["sw2T"] = _t(p["L0_ffn_sw2"])
    sh["WqkvT"] = _t(p["L1_attn_W_qkv"])
    sh["WprojT"] = _t(p["L1_attn_W_proj"])
    sh["bproj128"] = _bias128(p["L1_attn_b_proj"])
    sh["Wqkv6"] = np.ascontiguousarray(p["L1_ffn_Wqkv"], dtype=np.float32)
    sh["Wo6"] = np.ascontiguousarray(p["L1_ffn_Wo"], dtype=np.float32)
    W1 = np.asarray(p["L1_ffn_W1"], dtype=np.float32)
    W1p = np.zeros((6, 1024, 2 * FFN_IP), np.float32)
    W1p[:, :, :FFN_I] = W1[:, :, :FFN_I]
    W1p[:, :, FFN_IP:FFN_IP + FFN_I] = W1[:, :, FFN_I:]
    sh["W1p"] = W1p
    b1 = np.asarray(p["L1_ffn_b1"], dtype=np.float32)
    b1p = np.zeros((6, 2 * FFN_IP), np.float32)
    b1p[:, :FFN_I] = b1[:, :FFN_I]
    b1p[:, FFN_IP:FFN_IP + FFN_I] = b1[:, FFN_I:]
    sh["b1p128"] = np.stack([_bias128(b1p[i]) for i in range(6)])
    W2 = np.asarray(p["L1_ffn_W2"], dtype=np.float32)
    W2p = np.zeros((6, FFN_IP, 1024), np.float32)
    W2p[:, :FFN_I] = W2
    sh["W2p"] = W2p
    sh["b2128"] = np.stack([_bias128(np.asarray(p["L1_ffn_b2"])[i]) for i in range(6)])
    sh["flng128"] = np.stack([_bias128(np.asarray(p["L1_ffn_ln_g"])[i]) for i in range(6)])
    sh["flnb128"] = np.stack([_bias128(np.asarray(p["L1_ffn_ln_b"])[i]) for i in range(6)])
    qs = np.asarray(p["L1_ffn_q_scale"], dtype=np.float32)
    ks = np.asarray(p["L1_ffn_k_scale"], dtype=np.float32)
    sh["qs6"] = np.ascontiguousarray(np.tile(qs, (1, 2)).reshape(6, 128, 1))
    sh["ks6"] = np.ascontiguousarray(np.tile(ks, (1, 2)).reshape(6, 128, 1))
    sh["ident"] = np.eye(128, dtype=np.float32).astype(BF)
    sh["identf"] = np.eye(128, dtype=np.float32)
    ones2 = np.zeros((128, 2), np.float32)
    ones2[:64, 0] = 1.0
    ones2[64:, 1] = 1.0
    sh["ones2"] = ones2.astype(BF)
    sel2 = np.zeros((2, 128), np.float32)
    sel2[0, :64] = 1.0
    sel2[1, 64:] = 1.0
    sh["sel2f"] = sel2
    sh["masks"] = _band_masks()
    sh["csq"], sh["ssq"], sh["csk"], sh["ssk"] = _rotary_tables()
    x = np.asarray(p["L0_attn_x"], dtype=np.float32)
    percore = []
    for c in range(NCORES):
        xs = x[c * B_CORE:(c + 1) * B_CORE].reshape(TOK, D)
        percore.append({"x_fm": np.ascontiguousarray(xs.T)})
    return sh, percore


class Ctx:
    def __init__(self, nc):
        self.nc = nc
        self.stack = ExitStack()
        self.frames = [self.stack]
        self.sems = {}
        self.cnt = defaultdict(int)
        self.phase = 0
        self.gidc = {}
        self.R = {}

    def push_frame(self):
        self.frames.append(ExitStack())

    def pop_frame(self):
        assert len(self.frames) > 1, "attempt to pop the root frame"
        self.frames.pop().close()

    def sem(self, name):
        if name not in self.sems:
            self.sems[name] = self.stack.enter_context(self.nc.semaphore(name))
        return self.sems[name]

    def inc(self, inst, name, amt=1):
        inst.then_inc(self.sem(name), amt)
        self.cnt[name] += amt
        return self.cnt[name]

    def wait(self, eng, name, val=None):
        v = self.cnt[name] if val is None else val
        if v > 0:
            eng.wait_ge(self.sem(name), v)

    def sbuf(self, name, shape, dtype):
        return self.frames[-1].enter_context(self.nc.sbuf_tensor(name, shape, dtype))

    def psum(self, name, shape, dtype=F32):
        return self.frames[-1].enter_context(self.nc.psum_tensor(name, shape, dtype))

    def barrier(self):
        nc = self.nc
        self.phase += 1
        for s in list(self.cnt.keys()):
            if s != "phase":
                self.wait(nc.gpsimd, s)
        nc.gpsimd.sem_inc(self.sem("phase"), 1)
        self.cnt["phase"] = self.phase
        for eng in (nc.sync, nc.tensor, nc.vector, nc.scalar):
            self.wait(eng, "phase")

    def close(self):
        self.stack.close()


NROT = 8

class DmaGroup:
    def __init__(self, ctx, base):
        ctx.gidc[base] = ctx.gidc.get(base, 0) + 1
        self.ctx = ctx
        self.name = f"{base}{ctx.gidc[base] % NROT}"
        self.target = ctx.cnt[self.name]

    def ld(self, dst, src, **kw):
        self.target = self.ctx.inc(self.ctx.nc.sync.dma_start(dst, src, **kw), self.name, 16)
        return self.target

    def ld2(self, dst, src, **kw):
        self.target = self.ctx.inc(self.ctx.nc.scalar.dma_start(dst, src, **kw), self.name, 16)
        return self.target

    def st(self, dst, src, **kw):
        self.target = self.ctx.inc(self.ctx.nc.gpsimd.dma_start(dst, src, **kw), self.name, 16)
        return self.target

    def wait(self, eng):
        eng.wait_ge(self.ctx.sem(self.name), self.target)


SCRATCH = {
    "xn0f": ([D, TOK], F32),
    "xn0b": ([D, TOK], BF16),
    "xs_s": ([DI, B_CORE * (SEQ + 3)], F32),
    "sres": ([DI, TOK], F32),
    "xs2f": ([DI, B_CORE * (SEQ + 3)], F32),
    "xs2b": ([DI, TOK], BF16),
    "sa": ([DI, TOK], F32),
    "sb": ([DI, TOK], F32),
    "yg": ([DI, TOK], BF16),
    "p1": ([D, TOK], F32),
    "gfm": ([8, TOK], F32),
    "x1": ([D, TOK], F32),
    "uA": ([D, B_CORE * (SEQ + 1)], F32),
    "uB": ([D, B_CORE * (SEQ + 1)], F32),
    "xn1b": ([D, TOK], BF16),
    "q1s": ([D, TOK], BF16),
    "k1s": ([D, TOK], BF16),
    "v1s": ([D, TOK], BF16),
    "ao": ([D, TOK], BF16),
    "p2": ([D, TOK], F32),
    "qscr": ([D, TOK], BF16),
    "kscr": ([D, B_CORE * (WIN + SEQ)], BF16),
    "vscr": ([B_CORE * 16 * 18 * 128, 64], BF16),
    "xnf": ([D, TOK], BF16),
    "hgt": ([FFN_IP, TOK], BF16),
}

INPUTS = {
    "x_fm": ([D, TOK], F32),
    "w_in": ([D, 4096], F32), "b_in128": ([128, 32], F32),
    "convw128": ([128, 64], F32), "convb128": ([128, 16], F32),
    "w_dt": ([DI, DI], F32), "b_dt128": ([128, 16], F32),
    "dskip128": ([128, 16], F32),
    "w_out": ([DI, D], F32), "b_out128": ([128, 8], F32),
    "lng128": ([128, 8], F32), "lnb128": ([128, 8], F32),
    "gate_wT": ([D, 8], F32), "gate_b8": ([1, 8], F32),
    "w1T": ([8, D, 1024], F32), "w3T": ([8, D, 1024], F32), "w2T": ([8, 1024, D], F32),
    "sw1T": ([D, DI], F32), "sw3T": ([D, DI], F32), "sw2T": ([DI, D], F32),
    "WqkvT": ([D, 3072], F32), "WprojT": ([D, D], F32), "bproj128": ([128, 8], F32),
    "Wqkv6": ([6, D, 3072], F32), "Wo6": ([6, D, D], F32),
    "W1p": ([6, D, 2 * FFN_IP], F32), "b1p128": ([6, 128, 44], F32),
    "W2p": ([6, FFN_IP, D], F32), "b2128": ([6, 128, 8], F32),
    "flng128": ([6, 128, 8], F32), "flnb128": ([6, 128, 8], F32),
    "qs6": ([6, 128, 1], F32), "ks6": ([6, 128, 1], F32),
    "ident": ([128, 128], BF16), "identf": ([128, 128], F32),
    "ones2": ([128, 2], BF16),
    "sel2f": ([2, 128], F32),
    "masks": ([3, 128, 384], BF16),
    "csq": ([128, SEQ], BF16), "ssq": ([128, SEQ], BF16),
    "csk": ([128, SEQ], BF16), "ssk": ([128, SEQ], BF16),
}

def declare_tensors(nc, dbg=()):
    T = {}
    for name, (shape, dt) in INPUTS.items():
        T[name] = nc.dram_tensor(name, shape, dt, kind="ExternalInput").ap()
    for name, (shape, dt) in SCRATCH.items():
        kind = "ExternalOutput" if name in dbg else "Internal"
        T[name] = nc.dram_tensor(name, shape, dt, kind=kind).ap()
    T["out_fm"] = nc.dram_tensor("out_fm", [D, TOK], F32, kind="ExternalOutput").ap()
    return T


def emit_consts(ctx, T):
    g = DmaGroup(ctx, "ld")
    """Load resident const tiles; create ones tiles; zero scratch zero-cols."""
    nc = ctx.nc
    R = ctx.R
    for name, cols, dt in [
        ("ident", 128, BF16), ("identf", 128, F32), ("ones2", 2, BF16),
        ("masks", None, BF16),
        ("b_in128", 32, F32), ("convw128", 64, F32), ("convb128", 16, F32),
        ("b_dt128", 16, F32), ("dskip128", 16, F32), ("b_out128", 8, F32),
        ("lng128", 8, F32), ("lnb128", 8, F32), ("gate_b8", None, F32),
        ("bproj128", 8, F32),
        ("csq", SEQ, BF16), ("ssq", SEQ, BF16), ("csk", SEQ, BF16),
        ("ssk", SEQ, BF16),
    ]:
        if name == "masks":
            R["masks"] = ctx.sbuf("c_masks", [128, 3 * 384], BF16)
            for v in range(3):
                g.ld(R["masks"][:, v * 384:(v + 1) * 384], T["masks"][v])
        elif name == "gate_b8":
            R["gate_b8"] = ctx.sbuf("c_gate_b8", [1, 8], F32)
            g.ld(R["gate_b8"][:], T["gate_b8"])
        else:
            p = 2 if name == "sel2f" else 128
            R[name] = ctx.sbuf("c_" + name, [p, cols], dt)
            g.ld(R[name][:], T[name])
    R["sel2f"] = ctx.sbuf("c_sel2f", [2, 128], F32)
    g.ld(R["sel2f"][:], T["sel2f"])
    R["qs6"] = ctx.sbuf("c_qs6", [128, 6], F32)
    R["ks6"] = ctx.sbuf("c_ks6", [128, 6], F32)
    for i in range(6):
        g.ld(R["qs6"][:, i:i + 1], T["qs6"][i])
        g.ld(R["ks6"][:, i:i + 1], T["ks6"][i])
    for nm in ("b1p128", "b2128", "flng128", "flnb128"):
        cols = INPUTS[nm][0][2]
        R[nm] = ctx.sbuf("c_" + nm, [128, 6 * cols], F32)
        for i in range(6):
            g.ld(R[nm][:, i * cols:(i + 1) * cols], T[nm][i])
    ctx.constmap = {}
    cvals = [1e-5, 1.0, 1e-6, 1e-24]
    R["cconst"] = ctx.sbuf("c_cconst", [128, len(cvals)], F32)
    for i, v in enumerate(cvals):
        ctx.constmap[v] = i
        V(ctx, "memset", R["cconst"][:, i:i + 1], float(v))
    R["ones128b"] = ctx.sbuf("c_ones128b", [128, 1], BF16)
    R["ones128f"] = ctx.sbuf("c_ones128f", [128, 1], F32)
    R["ones1f"] = ctx.sbuf("c_ones1f", [1, 128], F32)
    R["zeros128"] = ctx.sbuf("c_zeros128", [128, WIN], F32)
    v0 = V(ctx, "memset", R["ones128b"][:], 1.0)
    v0 = V(ctx, "memset", R["ones128f"][:], 1.0)
    v0 = V(ctx, "memset", R["ones1f"][:], 1.0)
    v0 = V(ctx, "memset", R["zeros128"][:], 0.0)
    gs = DmaGroup(ctx, "st")
    ctx.wait(nc.gpsimd, "ev")
    for s in range(B_CORE):
        for ch in range(DI // 128):
            gs.st(T["xs_s"][ch * 128:(ch + 1) * 128, s * (SEQ + 3):s * (SEQ + 3) + 3],
                  R["zeros128"][:, 0:3])
            gs.st(T["xs2f"][ch * 128:(ch + 1) * 128, s * (SEQ + 3):s * (SEQ + 3) + 3],
                  R["zeros128"][:, 0:3])
        for ch in range(D // 128):
            gs.st(T["uA"][ch * 128:(ch + 1) * 128, s * (SEQ + 1):s * (SEQ + 1) + 1],
                  R["zeros128"][:, 0:1], allow_slow_non_contiguous=True)
            gs.st(T["uB"][ch * 128:(ch + 1) * 128, s * (SEQ + 1):s * (SEQ + 1) + 1],
                  R["zeros128"][:, 0:1], allow_slow_non_contiguous=True)
            gs.st(T["kscr"][ch * 128:(ch + 1) * 128, s * (WIN + SEQ):s * (WIN + SEQ) + WIN],
                  R["zeros128"][:, 0:WIN])
    zb = ctx.sbuf("c_zb", [128, 64], BF16)
    V(ctx, "memset", zb[:], 0.0)
    ctx.wait(nc.gpsimd, "ev")
    for b in range(B_CORE):
        for h in range(16):
            base = (b * 16 + h) * 18 * 128
            for j in range(2):
                gs.st(T["vscr"][base + j * 128: base + (j + 1) * 128, :], zb[:])
    ctx.barrier()


def load_weight_bf(ctx, name, dram_ap, K, F, colmap=None, stg=None):
    nc = ctx.nc
    nk = K // 128
    wb = ctx.sbuf(name + "_bf", [128, nk * F], BF16)
    if stg is None:
        stg = [ctx.sbuf(name + "_stg0", [128, F], F32),
               ctx.sbuf(name + "_stg1", [128, F], F32)]
    stream_weight_bf(ctx, wb, dram_ap, K, F, stg, colmap)
    return wb


def stream_weight_bf(ctx, wb, dram_ap, K, F, stg, colmap=None):
    nc = ctx.nc
    nk = K // 128
    c0, cw = (0, F) if colmap is None else colmap
    evs = {}
    ev0 = ctx.cnt["pp"]
    for k in range(nk):
        eng = nc.sync if k % 2 == 0 else nc.scalar
        ctx.wait(eng, "pp", evs[k - 2] if k >= 2 else ev0)
        g = DmaGroup(ctx, "ldB")
        if k % 2 == 0:
            g.ld(stg[k % 2][:, 0:cw], dram_ap[k * 128:(k + 1) * 128, c0:c0 + cw])
        else:
            g.ld2(stg[k % 2][:, 0:cw], dram_ap[k * 128:(k + 1) * 128, c0:c0 + cw])
        g.wait(nc.gpsimd)
        evs[k] = P(ctx, "tensor_copy", wb[:, k * F:k * F + cw], stg[k % 2][:, 0:cw])


def cbias(ctx, v, p=128):
    i = ctx.constmap[v]
    return ctx.R["cconst"][0:p, i:i + 1]


def V(ctx, name, *a, **k):
    ctx.wait(ctx.nc.vector, "ev")
    return ctx.inc(getattr(ctx.nc.vector, name)(*a, **k), "ev")


def S(ctx, name, *a, **k):
    ctx.wait(ctx.nc.scalar, "ac")
    return ctx.inc(getattr(ctx.nc.scalar, name)(*a, **k), "ac")


def P(ctx, name, *a, **k):
    ctx.wait(ctx.nc.gpsimd, "pp")
    return ctx.inc(getattr(ctx.nc.gpsimd, name)(*a, **k), "pp")


def bcast_cols(ctx, psum_dst, row_f32):
    nc = ctx.nc
    return nc.tensor.matmul(psum_dst, ctx.R["ones1f"][:], row_f32, start=True, stop=True)



def emit_outer_ln(ctx, T, src, dst_bf=None, dst_f32=None, dst_zerocol=False, pfx="ln"):
    nc = ctx.nc
    ctx.push_frame()
    nk = D // 128
    xb = [ctx.sbuf(f"{pfx}_x{p}", [128, nk * TT], F32) for p in range(2)]
    cb = [ctx.sbuf(f"{pfx}_c{p}", [128, nk * TT], BF16) for p in range(2)]
    sq = [ctx.sbuf(f"{pfx}_q{p}", [128, nk * TT], BF16) for p in range(2)]
    ot = [ctx.sbuf(f"{pfx}_o{p}", [128, nk * TT], F32) for p in range(2)]
    ob = [ctx.sbuf(f"{pfx}_ob{p}", [128, nk * TT], BF16) for p in range(2)]
    sc = ctx.sbuf(f"{pfx}_sc", [1, 8 * TT], F32)
    rb = [ctx.sbuf(f"{pfx}_rb{p}", [128, TT], F32) for p in range(2)]
    psA = [ctx.psum(f"{pfx}_psA{p}", [1, TT]) for p in range(2)]
    psB = [ctx.psum(f"{pfx}_psB{p}", [1, TT]) for p in range(2)]
    psM = [ctx.psum(f"{pfx}_psM{p}", [128, TT]) for p in range(2)]
    psR = [ctx.psum(f"{pfx}_psR{p}", [128, TT]) for p in range(2)]
    done_ev = {}
    st_grp = {}
    done_sq = {}
    done_pp = {}
    ones = ctx.R["ones128b"]
    for t in range(NTT):
        p = t % 2
        if t >= 2:
            ctx.wait(nc.sync, "ev", done_ev[t - 2])
            ctx.wait(nc.sync, "ac", done_sq[t - 2])
            ctx.wait(nc.sync, "pp", done_pp[t - 2])
        g = DmaGroup(ctx, "ld")
        for k in range(nk):
            g.ld(xb[p][:, k * TT:(k + 1) * TT],
                 src[k * 128:(k + 1) * 128, t * TT:(t + 1) * TT])
        g.wait(nc.vector)
        for k in range(nk):
            V(ctx, "tensor_copy", cb[p][:, k * TT:(k + 1) * TT],
                                          xb[p][:, k * TT:(k + 1) * TT])
        g.wait(nc.scalar)
        for k in range(nk):
            S(ctx, "activation", sq[p][:, k * TT:(k + 1) * TT],
                                         xb[p][:, k * TT:(k + 1) * TT],
                                         AF.Square, bias=0.0, scale=1.0)
        done_sq[t] = ctx.cnt["ac"]
        done_pp[t] = ctx.cnt["pp"]
        ctx.wait(nc.tensor, "pp")
        ctx.wait(nc.tensor, "ev")
        ctx.wait(nc.tensor, "ac")
        for k in range(nk):
            nc.tensor.matmul(psA[p][:], ones[:], cb[p][:, k * TT:(k + 1) * TT],
                             start=(k == 0), stop=(k == nk - 1))
        last = None
        for k in range(nk):
            last = nc.tensor.matmul(psB[p][:], ones[:], sq[p][:, k * TT:(k + 1) * TT],
                                    start=(k == 0), stop=(k == nk - 1))
        ctx.inc(last, "mm")
        mu = sc[:, (4 * p + 0) * TT:(4 * p + 1) * TT]
        vv = sc[:, (4 * p + 1) * TT:(4 * p + 2) * TT]
        ms = sc[:, (4 * p + 2) * TT:(4 * p + 3) * TT]
        rs = sc[:, (4 * p + 3) * TT:(4 * p + 4) * TT]
        ctx.wait(nc.scalar, "mm")
        S(ctx, "activation", mu, psA[p][:], AF.Copy, bias=0.0, scale=1.0 / D)
        S(ctx, "activation", vv, psB[p][:], AF.Copy, bias=0.0, scale=1.0 / D)
        S(ctx, "activation", ms, mu, AF.Square, bias=0.0, scale=1.0)
        ctx.wait(nc.vector, "ac")
        V(ctx, "tensor_tensor", vv, vv, ms, ALU.subtract)
        ctx.wait(nc.scalar, "ev")
        S(ctx, "activation", vv, vv, AF.Ln, bias=cbias(ctx, 1e-5, 1), scale=1.0)
        S(ctx, "activation", rs, vv, AF.Exp, bias=0.0, scale=-0.5)
        ctx.wait(nc.tensor, "ac")
        bcast_cols(ctx, psM[p][:], mu)
        ctx.inc(bcast_cols(ctx, psR[p][:], rs), "mm")
        ctx.wait(nc.vector, "mm")
        if t >= 2:
            st_grp[t - 2].wait(nc.vector)
        V(ctx, "tensor_copy", rb[p][:], psR[p][:])
        for k in range(nk):
            V(ctx, "tensor_tensor", ot[p][:, k * TT:(k + 1) * TT],
                                            xb[p][:, k * TT:(k + 1) * TT],
                                            psM[p][:], ALU.subtract)
            V(ctx, "tensor_tensor", ot[p][:, k * TT:(k + 1) * TT],
                                            ot[p][:, k * TT:(k + 1) * TT],
                                            rb[p][:], ALU.mult)
            if dst_bf is not None:
                V(ctx, "tensor_copy", ob[p][:, k * TT:(k + 1) * TT],
                                              ot[p][:, k * TT:(k + 1) * TT])
        done_ev[t] = ctx.cnt["ev"]
        ctx.wait(nc.gpsimd, "ev")
        gs = DmaGroup(ctx, "st")
        s, t0 = t // TPS, (t % TPS) * TT
        for k in range(nk):
            if dst_f32 is not None:
                col = s * (SEQ + 1) + 1 + t0 if dst_zerocol else t * TT
                gs.st(dst_f32[k * 128:(k + 1) * 128, col:col + TT],
                      ot[p][:, k * TT:(k + 1) * TT])
            if dst_bf is not None:
                gs.st(dst_bf[k * 128:(k + 1) * 128, t * TT:(t + 1) * TT],
                      ob[p][:, k * TT:(k + 1) * TT])
        st_grp[t] = gs
    ctx.barrier()
    ctx.pop_frame()



def emit_l0_inproj(ctx, T):
    nc = ctx.nc
    ctx.push_frame()
    nk = D // 128
    nf = 4096 // 128
    NS = 6
    wb = load_weight_bf(ctx, "w_in", T["w_in"], D, 4096)
    rhs = [ctx.sbuf(f"ip_r{p}", [128, nk * TT], BF16) for p in range(2)]
    ob = ctx.sbuf("ip_ob", [128, NS * TT], F32)
    ps = [ctx.psum(f"ip_ps{p}", [128, TT]) for p in range(4)]
    done_mm = {}
    slot_st = {}
    ac_base = ctx.cnt["ac"]
    gidx = 0
    for t in range(NTT):
        p = t % 2
        if t >= 2:
            ctx.wait(nc.sync, "mm", done_mm[t - 2])
        g = DmaGroup(ctx, "ld")
        for k in range(nk):
            g.ld(rhs[p][:, k * TT:(k + 1) * TT],
                 T["xn0b"][k * 128:(k + 1) * 128, t * TT:(t + 1) * TT])
        g.wait(nc.tensor)
        if t == 0:
            ctx.wait(nc.tensor, "pp")
        s, t0 = t // TPS, (t % TPS) * TT
        for f in range(nf):
            bank = ps[gidx % 4]
            ctx.wait(nc.tensor, "ac", max(0, ac_base + gidx - 3))
            last = None
            for k in range(nk):
                last = nc.tensor.matmul(bank[:], wb[:, k * 4096 + f * 128: k * 4096 + (f + 1) * 128],
                                        rhs[p][:, k * TT:(k + 1) * TT],
                                        start=(k == 0), stop=(k == nk - 1))
            ctx.inc(last, "mm")
            slot = gidx % NS
            ctx.wait(nc.scalar, "mm")
            if gidx >= NS:
                slot_st[gidx - NS].wait(nc.scalar)
            S(ctx, "activation", ob[:, slot * TT:(slot + 1) * TT], bank[:],
                                         AF.Silu, bias=ctx.R["b_in128"][:, f:f + 1],
                                         scale=1.0)
            ctx.wait(nc.gpsimd, "ac")
            gs = DmaGroup(ctx, "st")
            if f < 16:
                col = s * (SEQ + 3) + 3 + t0
                gs.st(T["xs_s"][f * 128:(f + 1) * 128, col:col + TT],
                      ob[:, slot * TT:(slot + 1) * TT])
            else:
                fr = f - 16
                gs.st(T["sres"][fr * 128:(fr + 1) * 128, t * TT:(t + 1) * TT],
                      ob[:, slot * TT:(slot + 1) * TT])
            slot_st[gidx] = gs
            gidx += 1
        done_mm[t] = ctx.cnt["mm"]
    ctx.barrier()
    ctx.pop_frame()



def emit_l0_conv(ctx, T):
    nc = ctx.nc
    ctx.push_frame()
    xin = [ctx.sbuf(f"cv_x{p}", [128, SEQ + 3], F32) for p in range(2)]
    acc = [ctx.sbuf(f"cv_a{p}", [128, SEQ], F32) for p in range(2)]
    oo = [ctx.sbuf(f"cv_o{p}", [128, SEQ], F32) for p in range(2)]
    ob = [ctx.sbuf(f"cv_ob{p}", [128, SEQ], BF16) for p in range(2)]
    done_ev = {}
    done_ac = {}
    st_grp = {}
    cw = ctx.R["convw128"]
    u = 0
    for s in range(B_CORE):
        for c in range(DI // 128):
            p = u % 2
            if u >= 2:
                ctx.wait(nc.sync, "ev", done_ev[u - 2])
                ctx.wait(nc.sync, "ac", done_ac[u - 2])
            g = DmaGroup(ctx, "ld")
            g.ld(xin[p][:], T["xs_s"][c * 128:(c + 1) * 128,
                                      s * (SEQ + 3):(s + 1) * (SEQ + 3)])
            g.wait(nc.vector)
            if u >= 2:
                ctx.wait(nc.vector, "ac", done_ac[u - 2])
            V(ctx, "tensor_scalar_mul", acc[p][:], xin[p][:, 0:SEQ], cw[:, c * 4:c * 4 + 1])
            for k in range(1, 4):
                V(ctx, "scalar_tensor_tensor", acc[p][:], xin[p][:, k:k + SEQ],
                  cw[:, c * 4 + k:c * 4 + k + 1], acc[p][:], ALU.mult, ALU.add)
            done_ev[u] = ctx.cnt["ev"]
            ctx.wait(nc.scalar, "ev")
            if u >= 2:
                ctx.wait(nc.scalar, "ev", done_ev[u - 2])
                st_grp[u - 2].wait(nc.scalar)
            S(ctx, "activation", oo[p][:], acc[p][:], AF.Silu,
              bias=ctx.R["convb128"][:, c:c + 1], scale=1.0)
            done_ac[u] = ctx.cnt["ac"]
            ctx.wait(nc.vector, "ac")
            if u >= 2:
                st_grp[u - 2].wait(nc.vector)
            V(ctx, "tensor_copy", ob[p][:], oo[p][:])
            ctx.wait(nc.gpsimd, "ev")
            gs = DmaGroup(ctx, "st")
            gs.st(T["xs2f"][c * 128:(c + 1) * 128,
                            s * (SEQ + 3) + 3:(s + 1) * (SEQ + 3)], oo[p][:])
            gs.st(T["xs2b"][c * 128:(c + 1) * 128, s * SEQ:(s + 1) * SEQ], ob[p][:])
            st_grp[u] = gs
            u += 1
    ctx.barrier()
    ctx.pop_frame()



def emit_l0_dt(ctx, T):
    nc = ctx.nc
    ctx.push_frame()
    nk = DI // 128
    nf = DI // 128
    wb = load_weight_bf(ctx, "w_dt", T["w_dt"], DI, DI)
    rhs = [ctx.sbuf(f"dt_r{p}", [128, nk * TT], BF16) for p in range(2)]
    xfr = ctx.sbuf("dt_xf", [128, 4 * TT], F32)
    t1 = ctx.sbuf("dt_t1", [128, TT], F32)
    t2 = ctx.sbuf("dt_t2", [128, TT], F32)
    NS = 4
    oa = ctx.sbuf("dt_oa", [128, NS * TT], F32)
    obb = ctx.sbuf("dt_ob", [128, NS * TT], F32)
    ps = [ctx.psum(f"dt_ps{p}", [128, TT]) for p in range(4)]
    done_mm = {}
    slot_st = {}
    pr_ac = {}
    xf_ev = {}
    gidx = 0
    for t in range(NTT):
        p = t % 2
        s, t0 = t // TPS, (t % TPS) * TT
        if t >= 2:
            ctx.wait(nc.sync, "mm", done_mm[t - 2])
        g = DmaGroup(ctx, "ld")
        for k in range(nk):
            g.ld(rhs[p][:, k * TT:(k + 1) * TT],
                 T["xs2b"][k * 128:(k + 1) * 128, t * TT:(t + 1) * TT])
        g.wait(nc.tensor)
        if t == 0:
            ctx.wait(nc.tensor, "pp")
        for f in range(nf):
            bank = ps[gidx % 4]
            xslot = gidx % 4
            if gidx >= 4:
                ctx.wait(nc.sync, "ev", xf_ev[gidx - 4])
            gx = DmaGroup(ctx, "ldC")
            gx.ld(xfr[:, xslot * TT:(xslot + 1) * TT],
                  T["xs2f"][f * 128:(f + 1) * 128,
                            s * (SEQ + 3) + 3 + t0:s * (SEQ + 3) + 3 + t0 + TT])
            if gidx >= 4:
                ctx.wait(nc.tensor, "ac", pr_ac[gidx - 4])
            last = None
            for k in range(nk):
                last = nc.tensor.matmul(bank[:], wb[:, k * DI + f * 128: k * DI + (f + 1) * 128],
                                        rhs[p][:, k * TT:(k + 1) * TT],
                                        start=(k == 0), stop=(k == nk - 1))
            ctx.inc(last, "mm")
            slot = gidx % NS
            bdt = ctx.R["b_dt128"][:, f:f + 1]
            ctx.wait(nc.scalar, "mm")
            ctx.wait(nc.scalar, "ev")
            S(ctx, "activation", t1[:], bank[:], AF.Abs, bias=bdt, scale=1.0)
            S(ctx, "activation", t1[:], t1[:], AF.Exp, bias=0.0, scale=-1.0)
            S(ctx, "activation", t1[:], t1[:], AF.Ln, bias=cbias(ctx, 1.0), scale=1.0)
            S(ctx, "activation", t2[:], bank[:], AF.Relu, bias=bdt, scale=1.0)
            pr_ac[gidx] = ctx.cnt["ac"]
            ctx.wait(nc.vector, "ac")
            V(ctx, "tensor_tensor", t2[:], t2[:], t1[:], ALU.add)
            ctx.wait(nc.scalar, "ev")
            S(ctx, "activation", t1[:], t2[:], AF.Exp, bias=0.0, scale=-1.0)
            if gidx >= NS:
                slot_st[gidx - NS].wait(nc.vector)
            ctx.wait(nc.vector, "ac")
            V(ctx, "tensor_scalar_add", t2[:], t1[:], 1.0)
            V(ctx, "reciprocal", t2[:], t2[:])
            V(ctx, "tensor_tensor", oa[:, slot * TT:(slot + 1) * TT], t1[:], t2[:], ALU.mult)
            gx.wait(nc.vector)
            V(ctx, "tensor_tensor", obb[:, slot * TT:(slot + 1) * TT], t2[:],
              xfr[:, xslot * TT:(xslot + 1) * TT], ALU.mult)
            xf_ev[gidx] = ctx.cnt["ev"]
            ctx.wait(nc.gpsimd, "ev")
            gs = DmaGroup(ctx, "st")
            gs.st(T["sa"][f * 128:(f + 1) * 128, t * TT:(t + 1) * TT],
                  oa[:, slot * TT:(slot + 1) * TT])
            gs.st(T["sb"][f * 128:(f + 1) * 128, t * TT:(t + 1) * TT],
                  obb[:, slot * TT:(slot + 1) * TT])
            slot_st[gidx] = gs
            gidx += 1
        done_mm[t] = ctx.cnt["mm"]
    ctx.barrier()
    ctx.pop_frame()



def emit_l0_scan(ctx, T):
    nc = ctx.nc
    ctx.push_frame()
    sa = [ctx.sbuf(f"sc_a{p}", [128, SEQ], F32) for p in range(2)]
    sb = [ctx.sbuf(f"sc_b{p}", [128, SEQ], F32) for p in range(2)]
    xf = [ctx.sbuf(f"sc_x{p}", [128, SEQ], F32) for p in range(2)]
    sr = [ctx.sbuf(f"sc_r{p}", [128, SEQ], F32) for p in range(2)]
    yb = [ctx.sbuf(f"sc_y{p}", [128, SEQ], BF16) for p in range(2)]
    done_ev = {}
    st_grp = {}
    u = 0
    for s in range(B_CORE):
        for c in range(DI // 128):
            p = u % 2
            if u >= 2:
                ctx.wait(nc.sync, "ev", done_ev[u - 2])
            g = DmaGroup(ctx, "ld")
            g.ld(sa[p][:], T["sa"][c * 128:(c + 1) * 128, s * SEQ:(s + 1) * SEQ])
            g.ld(sb[p][:], T["sb"][c * 128:(c + 1) * 128, s * SEQ:(s + 1) * SEQ])
            g.ld(xf[p][:], T["xs2f"][c * 128:(c + 1) * 128,
                                     s * (SEQ + 3) + 3:(s + 1) * (SEQ + 3)])
            g.ld(sr[p][:], T["sres"][c * 128:(c + 1) * 128, s * SEQ:(s + 1) * SEQ])
            g.wait(nc.vector)
            V(ctx, "memset", sa[p][:, 0:1], 0.0)
            V(ctx, "tensor_copy", sb[p][:, 0:1], xf[p][:, 0:1])
            V(ctx, "tensor_tensor_scan", sb[p][:], sa[p][:], sb[p][:], 0.0,
              ALU.mult, ALU.add)
            V(ctx, "scalar_tensor_tensor", sb[p][:], xf[p][:],
              ctx.R["dskip128"][:, c:c + 1], sb[p][:], ALU.mult, ALU.add)
            if u >= 2:
                st_grp[u - 2].wait(nc.vector)
            V(ctx, "tensor_tensor", yb[p][:], sb[p][:], sr[p][:], ALU.mult)
            done_ev[u] = ctx.cnt["ev"]
            ctx.wait(nc.gpsimd, "ev")
            gs = DmaGroup(ctx, "st")
            gs.st(T["yg"][c * 128:(c + 1) * 128, s * SEQ:(s + 1) * SEQ], yb[p][:])
            st_grp[u] = gs
            u += 1
    ctx.barrier()
    ctx.pop_frame()



def emit_l0_outproj(ctx, T):
    nc = ctx.nc
    ctx.push_frame()
    nk = DI // 128
    nf = D // 128
    wb = load_weight_bf(ctx, "w_out", T["w_out"], DI, D)
    rhs = [ctx.sbuf(f"op_r{p}", [128, nk * TT], BF16) for p in range(2)]
    xnr = ctx.sbuf("op_xn", [128, 4 * TT], F32)
    xt = [ctx.sbuf(f"op_x{p}", [128, nf * TT], F32) for p in range(2)]
    h2 = [ctx.sbuf(f"op_h{p}", [128, nf * TT], F32) for p in range(2)]
    sq = [ctx.sbuf(f"op_q{p}", [128, nf * TT], BF16) for p in range(2)]
    sc = ctx.sbuf("op_sc", [1, 8 * TT], F32)
    rb = [ctx.sbuf(f"op_rb{p}", [128, TT], F32) for p in range(2)]
    ps = [ctx.psum(f"op_ps{p}", [128, TT]) for p in range(2)]
    psA = [ctx.psum("op_psA", [1, TT])] * 2
    psB = [ctx.psum("op_psB", [1, TT])] * 2
    psM = [ctx.psum("op_psM", [128, TT])] * 2
    psR = [ctx.psum("op_psR", [128, TT])] * 2
    ones = ctx.R["ones128b"]
    done_ev = {}
    done_mm = {}
    done_sq = {}
    st_grp = {}
    xn_ev = {}
    for t in range(NTT):
        p = t % 2
        if t >= 2:
            ctx.wait(nc.sync, "mm", done_mm[t - 2])
            ctx.wait(nc.sync, "ev", done_ev[t - 2])
            ctx.wait(nc.sync, "ac", done_sq[t - 2])
        g = DmaGroup(ctx, "ld")
        for k in range(nk):
            g.ld(rhs[p][:, k * TT:(k + 1) * TT],
                 T["yg"][k * 128:(k + 1) * 128, t * TT:(t + 1) * TT])
        for f in range(nf):
            g.ld(xt[p][:, f * TT:(f + 1) * TT],
                 T["x_fm"][f * 128:(f + 1) * 128, t * TT:(t + 1) * TT])
        g.wait(nc.tensor)
        if t == 0:
            ctx.wait(nc.tensor, "pp")
        g.wait(nc.vector)
        g.wait(nc.scalar)
        if t >= 2:
            st_grp[t - 2].wait(nc.vector)
        for f in range(nf):
            bank = ps[f % 2]
            gidx5 = t * nf + f
            xslot = gidx5 % 4
            if gidx5 >= 4:
                ctx.wait(nc.sync, "ev", xn_ev[gidx5 - 4])
            gx = DmaGroup(ctx, "ldC")
            gx.ld(xnr[:, xslot * TT:(xslot + 1) * TT],
                  T["xn0f"][f * 128:(f + 1) * 128, t * TT:(t + 1) * TT])
            if gidx5 >= 2:
                ctx.wait(nc.tensor, "ev", xn_ev[gidx5 - 2])
            last = None
            for k in range(nk):
                last = nc.tensor.matmul(bank[:], wb[:, k * D + f * 128: k * D + (f + 1) * 128],
                                        rhs[p][:, k * TT:(k + 1) * TT],
                                        start=(k == 0), stop=(k == nk - 1))
            ctx.inc(last, "mm")
            ctx.wait(nc.vector, "mm")
            gx.wait(nc.vector)
            V(ctx, "scalar_tensor_tensor", h2[p][:, f * TT:(f + 1) * TT], bank[:],
              ctx.R["b_out128"][:, f:f + 1], xnr[:, xslot * TT:(xslot + 1) * TT],
              ALU.add, ALU.add)
            xn_ev[gidx5] = ctx.cnt["ev"]
            ctx.wait(nc.scalar, "ev")
            S(ctx, "activation", sq[p][:, f * TT:(f + 1) * TT],
              h2[p][:, f * TT:(f + 1) * TT], AF.Square, bias=0.0, scale=1.0)
        done_sq[t] = ctx.cnt["ac"]
        ctx.wait(nc.tensor, "ev")
        ctx.wait(nc.tensor, "ac")
        for k in range(nf):
            nc.tensor.matmul(psA[p][:], ctx.R["ones128f"][:], h2[p][:, k * TT:(k + 1) * TT],
                             start=(k == 0), stop=(k == nf - 1))
        last = None
        for k in range(nf):
            last = nc.tensor.matmul(psB[p][:], ones[:], sq[p][:, k * TT:(k + 1) * TT],
                                    start=(k == 0), stop=(k == nf - 1))
        ctx.inc(last, "mm")
        mu = sc[:, (4 * p + 0) * TT:(4 * p + 1) * TT]
        vv = sc[:, (4 * p + 1) * TT:(4 * p + 2) * TT]
        ms = sc[:, (4 * p + 2) * TT:(4 * p + 3) * TT]
        rs = sc[:, (4 * p + 3) * TT:(4 * p + 4) * TT]
        ctx.wait(nc.scalar, "mm")
        S(ctx, "activation", mu, psA[p][:], AF.Copy, bias=0.0, scale=1.0 / D)
        S(ctx, "activation", vv, psB[p][:], AF.Copy, bias=0.0, scale=1.0 / D)
        S(ctx, "activation", ms, mu, AF.Square, bias=0.0, scale=1.0)
        ctx.wait(nc.vector, "ac")
        V(ctx, "tensor_tensor", vv, vv, ms, ALU.subtract)
        ctx.wait(nc.scalar, "ev")
        S(ctx, "activation", vv, vv, AF.Ln, bias=cbias(ctx, 1e-5, 1), scale=1.0)
        S(ctx, "activation", rs, vv, AF.Exp, bias=0.0, scale=-0.5)
        ctx.wait(nc.tensor, "ac")
        bcast_cols(ctx, psM[p][:], mu)
        ctx.inc(bcast_cols(ctx, psR[p][:], rs), "mm")
        ctx.wait(nc.vector, "mm")
        V(ctx, "tensor_copy", rb[p][:], psR[p][:])
        for f in range(nf):
            V(ctx, "tensor_tensor", h2[p][:, f * TT:(f + 1) * TT],
              h2[p][:, f * TT:(f + 1) * TT], psM[p][:], ALU.subtract)
            V(ctx, "tensor_tensor", h2[p][:, f * TT:(f + 1) * TT],
              h2[p][:, f * TT:(f + 1) * TT], rb[p][:], ALU.mult)
        for f in range(nf):
            V(ctx, "tensor_scalar", h2[p][:, f * TT:(f + 1) * TT],
              h2[p][:, f * TT:(f + 1) * TT], ctx.R["lng128"][:, f:f + 1],
              ctx.R["lnb128"][:, f:f + 1], ALU.mult, ALU.add)
        for f in range(nf):
            V(ctx, "tensor_tensor", h2[p][:, f * TT:(f + 1) * TT],
              h2[p][:, f * TT:(f + 1) * TT], xt[p][:, f * TT:(f + 1) * TT], ALU.add)
        done_ev[t] = ctx.cnt["ev"]
        done_mm[t] = ctx.cnt["mm"]
        ctx.wait(nc.gpsimd, "ev")
        gs = DmaGroup(ctx, "st")
        for f in range(nf):
            gs.st(T["p1"][f * 128:(f + 1) * 128, t * TT:(t + 1) * TT],
                  h2[p][:, f * TT:(f + 1) * TT])
        st_grp[t] = gs
    ctx.barrier()
    ctx.pop_frame()



def emit_moe_gate(ctx, T):
    nc = ctx.nc
    ctx.push_frame()
    nk = D // 128
    wgb = load_weight_bf(ctx, "gate_wT", T["gate_wT"], D, 8)
    rhs = [ctx.sbuf(f"gt_r{p}", [128, nk * TT], BF16) for p in range(2)]
    e8 = ctx.sbuf("gt_e8", [128, 8], F32)
    s8 = ctx.sbuf("gt_s8", [128, 8], F32)
    t8 = ctx.sbuf("gt_t8", [128, 8], F32)
    g8 = ctx.sbuf("gt_g8", [128, 8], F32)
    m8 = ctx.sbuf("gt_m8", [128, 4], F32)
    gb = ctx.sbuf("gt_gb", [128, 8], F32)
    go = [ctx.sbuf(f"gt_go{p}", [8, 4 * 128], F32) for p in range(2)]
    ps = [ctx.psum(f"gt_ps{q}", [128, 8]) for q in range(2)]
    pst = ctx.psum("gt_pst", [8, 128])
    psg = ctx.psum("gt_psg", [128, 8])
    ctx.inc(nc.tensor.matmul(psg[:], ctx.R["ones1f"][:], ctx.R["gate_b8"][:],
                             start=True, stop=True), "mm")
    ctx.wait(nc.vector, "mm")
    V(ctx, "tensor_copy", gb[:], psg[:])
    done_mm = {}
    st_grp = {}
    for t in range(NTT):
        p = t % 2
        if t >= 2:
            ctx.wait(nc.sync, "mm", done_mm[t - 2])
        g = DmaGroup(ctx, "ld")
        for k in range(nk):
            g.ld(rhs[p][:, k * TT:(k + 1) * TT],
                 T["xn0b"][k * 128:(k + 1) * 128, t * TT:(t + 1) * TT])
        g.wait(nc.tensor)
        if t == 0:
            ctx.wait(nc.tensor, "pp")
        if t >= 2:
            st_grp[t - 2].wait(nc.vector)
        for c4 in range(4):
            q = c4 % 2
            ctx.wait(nc.tensor, "ev")
            last = None
            for k in range(nk):
                last = nc.tensor.matmul(ps[q][:],
                                        rhs[p][:, k * TT + c4 * 128:k * TT + (c4 + 1) * 128],
                                        wgb[:, k * 8:(k + 1) * 8],
                                        start=(k == 0), stop=(k == nk - 1))
            ctx.inc(last, "mm")
            ctx.wait(nc.vector, "mm")
            V(ctx, "tensor_reduce", m8[:, 0:1], ps[q][:], axis=AX.X, op=ALU.max,
              negate=True)
            ctx.wait(nc.scalar, "ev")
            S(ctx, "activation", e8[:], ps[q][:], AF.Exp, bias=m8[:, 0:1], scale=1.0,
              accum_out=m8[:, 1:2])
            ctx.wait(nc.vector, "ac")
            V(ctx, "reciprocal", m8[:, 1:2], m8[:, 1:2])
            V(ctx, "tensor_scalar_mul", s8[:], e8[:], m8[:, 1:2])
            V(ctx, "tensor_tensor", t8[:], s8[:], gb[:], ALU.add)
            V(ctx, "tensor_reduce", m8[:, 2:3], t8[:], axis=AX.X, op=ALU.max)
            V(ctx, "tensor_single_scalar", g8[:], t8[:], m8[:, 2:3], ALU.is_ge)
            V(ctx, "scalar_tensor_tensor", g8[:], g8[:], -3.4e38, t8[:],
              ALU.mult, ALU.add)
            V(ctx, "tensor_reduce", m8[:, 3:4], g8[:], axis=AX.X, op=ALU.max)
            V(ctx, "tensor_single_scalar", g8[:], t8[:], m8[:, 3:4], ALU.is_ge)
            V(ctx, "tensor_tensor", g8[:], g8[:], s8[:], ALU.mult)
            ctx.wait(nc.tensor, "ev")
            ctx.inc(nc.tensor.transpose(pst[:], g8[:], ctx.R["identf"][:]), "mm")
            ctx.wait(nc.vector, "mm")
            V(ctx, "tensor_copy", go[p][:, c4 * 128:(c4 + 1) * 128], pst[:])
        done_mm[t] = ctx.cnt["mm"]
        ctx.wait(nc.gpsimd, "ev")
        gs = DmaGroup(ctx, "st")
        gs.st(T["gfm"][:, t * TT:(t + 1) * TT], go[p][:])
        st_grp[t] = gs
    ctx.barrier()
    ctx.pop_frame()



def emit_moe_expert(ctx, T, e):
    nc = ctx.nc
    ctx.push_frame()
    shared = (e == -1)
    NI = 16 if shared else 8
    nk = D // 128
    nf = D // 128
    FI = DI if shared else 1024
    nm = f"mx{e}_"
    stg = [ctx.sbuf(nm + "stg0", [128, FI], F32), ctx.sbuf(nm + "stg1", [128, FI], F32)]
    if shared:
        w1b = load_weight_bf(ctx, "sw1T", T["sw1T"], D, DI, stg=stg)
        w3b = load_weight_bf(ctx, "sw3T", T["sw3T"], D, DI, stg=stg)
        w2b = load_weight_bf(ctx, "sw2T", T["sw2T"], DI, D, stg=stg)
    else:
        w1b = load_weight_bf(ctx, f"w1T{e}", T["w1T"][e], D, 1024, stg=stg)
        w3b = load_weight_bf(ctx, f"w3T{e}", T["w3T"][e], D, 1024, stg=stg)
        w2b = load_weight_bf(ctx, f"w2T{e}", T["w2T"][e], 1024, D, stg=stg)
    rhs = [ctx.sbuf(nm + f"r{p}", [128, nk * TT], BF16) for p in range(2)]
    hsc = ctx.sbuf(nm + "h", [128, NI * TT], BF16)
    x1t = ctx.sbuf(nm + "x1", [128, nf * TT], F32)
    s1 = ctx.sbuf(nm + "s1", [128, TT], F32)
    s2 = ctx.sbuf(nm + "s2", [128, TT], F32)
    gbs = ctx.sbuf(nm + "gb", [128, TT], F32)
    grow = ctx.sbuf(nm + "gr", [1, 2 * TT], F32)
    ph = [ctx.psum(nm + f"ph{q}", [128, TT]) for q in range(4)]
    po = [ctx.psum(nm + f"po{q}", [128, TT]) for q in range(2)]
    psg = ctx.psum(nm + "psg", [128, TT])
    done_mm = {}
    st_grp = {}
    gl_ev = {}
    h_ev = {}
    z_ev = {}
    for t in range(NTT):
        p = t % 2
        if t >= 2:
            ctx.wait(nc.sync, "mm", done_mm[t - 2])
        g = DmaGroup(ctx, "ld")
        for k in range(nk):
            g.ld(rhs[p][:, k * TT:(k + 1) * TT],
                 T["xn0b"][k * 128:(k + 1) * 128, t * TT:(t + 1) * TT])
        if not shared:
            if t >= 2:
                ctx.wait(nc.sync, "ev", gl_ev[t - 2])
            gg = DmaGroup(ctx, "ldC")
            gg.ld(grow[:, (t % 2) * TT:(t % 2 + 1) * TT],
                  T["gfm"][e:e + 1, t * TT:(t + 1) * TT])
        g.wait(nc.tensor)
        if t == 0:
            ctx.wait(nc.tensor, "pp")
        if not shared:
            gg.wait(nc.tensor)
            if t >= 1:
                ctx.wait(nc.tensor, "ev", gl_ev[t - 1])
            ctx.inc(bcast_cols(ctx, psg[:], grow[:, (t % 2) * TT:(t % 2 + 1) * TT]), "mm")
            ctx.wait(nc.vector, "mm")
            V(ctx, "tensor_copy", gbs[:], psg[:])
            gl_ev[t] = ctx.cnt["ev"]
        for ic in range(NI):
            bh1 = ph[(ic % 2) * 2]
            bh3 = ph[(ic % 2) * 2 + 1]
            gic = t * NI + ic
            if gic >= 2:
                ctx.wait(nc.tensor, "ev", h_ev[gic - 2])
            for k in range(nk):
                nc.tensor.matmul(bh1[:], w1b[:, k * FI + ic * 128:k * FI + (ic + 1) * 128],
                                 rhs[p][:, k * TT:(k + 1) * TT],
                                 start=(k == 0), stop=(k == nk - 1))
            last = None
            for k in range(nk):
                last = nc.tensor.matmul(bh3[:], w3b[:, k * FI + ic * 128:k * FI + (ic + 1) * 128],
                                        rhs[p][:, k * TT:(k + 1) * TT],
                                        start=(k == 0), stop=(k == nk - 1))
            ctx.inc(last, "mm")
            ctx.wait(nc.scalar, "mm")
            ctx.wait(nc.scalar, "ev")
            S(ctx, "activation", s1[:], bh1[:], AF.Silu, bias=0.0, scale=1.0)
            ctx.wait(nc.vector, "ac")
            V(ctx, "tensor_tensor", s2[:], s1[:], bh3[:], ALU.mult)
            if shared:
                V(ctx, "tensor_copy", hsc[:, ic * TT:(ic + 1) * TT], s2[:])
            else:
                V(ctx, "tensor_tensor", hsc[:, ic * TT:(ic + 1) * TT], s2[:],
                  gbs[:], ALU.mult)
            h_ev[gic] = ctx.cnt["ev"]
        if t >= 1:
            st_grp[t - 1].wait(nc.sync)
        gx = DmaGroup(ctx, "ldC")
        for f in range(nf):
            gx.ld(x1t[:, f * TT:(f + 1) * TT],
                  T["p1" if shared else "x1"][f * 128:(f + 1) * 128, t * TT:(t + 1) * TT])
        gx.wait(nc.vector)
        hsc_ready = ctx.cnt["ev"]
        for f in range(nf):
            bank = po[f % 2]
            gzf = t * nf + f
            ctx.wait(nc.tensor, "ev", max(hsc_ready, z_ev.get(gzf - 2, 0)))
            last = None
            for k in range(NI):
                last = nc.tensor.matmul(bank[:], w2b[:, k * D + f * 128:k * D + (f + 1) * 128],
                                        hsc[:, k * TT:(k + 1) * TT],
                                        start=(k == 0), stop=(k == NI - 1))
            ctx.inc(last, "mm")
            ctx.wait(nc.vector, "mm")
            V(ctx, "tensor_tensor", x1t[:, f * TT:(f + 1) * TT],
              x1t[:, f * TT:(f + 1) * TT], bank[:], ALU.add)
            z_ev[gzf] = ctx.cnt["ev"]
        done_mm[t] = ctx.cnt["mm"]
        ctx.wait(nc.gpsimd, "ev")
        gs = DmaGroup(ctx, "st")
        for f in range(nf):
            gs.st(T["x1"][f * 128:(f + 1) * 128, t * TT:(t + 1) * TT],
                  x1t[:, f * TT:(f + 1) * TT])
        st_grp[t] = gs
    ctx.barrier()
    ctx.pop_frame()



def emit_l1a_qkv(ctx, T):
    nc = ctx.nc
    ctx.push_frame()
    nk = D // 128
    nf = 3072 // 128
    wb = load_weight_bf(ctx, "WqkvT", T["WqkvT"], D, 3072)
    rhs = [ctx.sbuf(f"qk_r{p}", [128, nk * TT], BF16) for p in range(2)]
    NS = 6
    tq = ctx.sbuf("qk_t", [128, TT], F32)
    ob = ctx.sbuf("qk_ob", [128, NS * TT], BF16)
    ps = [ctx.psum(f"qk_ps{q}", [128, TT]) for q in range(4)]
    done_mm = {}
    slot_st = {}
    ev_done = {}
    ac_done = {}
    gidx = 0
    for t in range(NTT):
        p = t % 2
        if t >= 2:
            ctx.wait(nc.sync, "mm", done_mm[t - 2])
        g = DmaGroup(ctx, "ld")
        for k in range(nk):
            g.ld(rhs[p][:, k * TT:(k + 1) * TT],
                 T["xn1b"][k * 128:(k + 1) * 128, t * TT:(t + 1) * TT])
        g.wait(nc.tensor)
        if t == 0:
            ctx.wait(nc.tensor, "pp")
        for f in range(nf):
            bank = ps[gidx % 4]
            if gidx >= 4:
                ctx.wait(nc.tensor, "ev", ev_done.get(gidx - 4, 0))
                ctx.wait(nc.tensor, "ac", ac_done.get(gidx - 4, 0))
            last = None
            for k in range(nk):
                last = nc.tensor.matmul(bank[:], wb[:, k * 3072 + f * 128: k * 3072 + (f + 1) * 128],
                                        rhs[p][:, k * TT:(k + 1) * TT],
                                        start=(k == 0), stop=(k == nk - 1))
            ctx.inc(last, "mm")
            slot = gidx % NS
            if gidx >= NS:
                if f < 16:
                    slot_st[gidx - NS].wait(nc.scalar)
                slot_st[gidx - NS].wait(nc.vector)
            if f < 16:
                ctx.wait(nc.scalar, "mm")
                ctx.wait(nc.scalar, "ev")
                S(ctx, "activation", tq[:], bank[:], AF.Square, bias=0.0, scale=1.0)
                S(ctx, "activation", ob[:, slot * TT:(slot + 1) * TT], tq[:],
                  AF.Exp, bias=0.0, scale=-0.5)
                ctx.wait(nc.gpsimd, "ac")
            else:
                ctx.wait(nc.vector, "mm")
                V(ctx, "tensor_copy", ob[:, slot * TT:(slot + 1) * TT], bank[:])
                ctx.wait(nc.gpsimd, "ev")
            ev_done[gidx] = ctx.cnt["ev"]
            ac_done[gidx] = ctx.cnt["ac"]
            gs = DmaGroup(ctx, "st")
            dst = T["q1s"] if f < 8 else (T["k1s"] if f < 16 else T["v1s"])
            fr = f % 8
            gs.st(dst[fr * 128:(fr + 1) * 128, t * TT:(t + 1) * TT],
                  ob[:, slot * TT:(slot + 1) * TT])
            slot_st[gidx] = gs
            gidx += 1
        done_mm[t] = ctx.cnt["mm"]
    ctx.barrier()
    ctx.pop_frame()



def emit_l1a_core(ctx, T):
    nc = ctx.nc
    ctx.push_frame()
    qsl = [ctx.sbuf(f"la_q{p}", [64, SEQ], BF16) for p in range(2)]
    ksl = [ctx.sbuf(f"la_k{p}", [64, SEQ], BF16) for p in range(2)]
    vsl = [ctx.sbuf(f"la_v{p}", [64, SEQ], BF16) for p in range(2)]
    ktm = ctx.sbuf("la_ktm", [128, 16 * 64], BF16)
    vtm = ctx.sbuf("la_vtm", [128, 16 * 65], BF16)
    kvb = ctx.sbuf("la_kvb", [64, 65], BF16)
    dd = ctx.sbuf("la_dd", [128, 1], F32)
    otm = ctx.sbuf("la_otm", [128, 64], BF16)
    aob = [ctx.sbuf(f"la_ao{p}", [64, SEQ], BF16) for p in range(2)]
    pt = [ctx.psum(f"la_pt{q}", [128, 64], BF16) for q in range(2)]
    pkv = ctx.psum("la_pkv", [64, 65], F32)
    po = [ctx.psum(f"la_po{q}", [128, 65], F32) for q in range(2)]
    pt2 = [ctx.psum(f"la_pt2{q}", [64, 128], BF16) for q in range(2)]
    idn = ctx.R["ident"]
    done_mm = {}
    st_grp = {}
    tp_ev = {}
    o_ev = {}
    u = 0
    for b in range(B_CORE):
        for h in range(16):
            p = u % 2
            if u >= 2:
                ctx.wait(nc.sync, "mm", done_mm[u - 2])
            g = DmaGroup(ctx, "ld")
            g.ld(qsl[p][:], T["q1s"][h * 64:(h + 1) * 64, b * SEQ:(b + 1) * SEQ])
            g.ld(ksl[p][:], T["k1s"][h * 64:(h + 1) * 64, b * SEQ:(b + 1) * SEQ])
            g.ld(vsl[p][:], T["v1s"][h * 64:(h + 1) * 64, b * SEQ:(b + 1) * SEQ])
            ctx.wait(nc.vector, "mm")
            V(ctx, "memset", vtm[:], 1.0)
            g.wait(nc.tensor)
            ctx.wait(nc.tensor, "ev")
            for c in range(16):
                for which in range(2):
                    gtr = u * 32 + c * 2 + which
                    if gtr >= 2:
                        ctx.wait(nc.tensor, "ev", tp_ev.get(gtr - 2, 0))
                    srcv = ksl[p][:, c * 128:(c + 1) * 128] if which == 0 else \
                        vsl[p][:, c * 128:(c + 1) * 128]
                    ctx.inc(nc.tensor.transpose(pt[gtr % 2][:], srcv,
                                                idn[0:64, 0:64]), "mm")
                    ctx.wait(nc.vector, "mm")
                    dstv = ktm[:, c * 64:(c + 1) * 64] if which == 0 else \
                        vtm[:, c * 65:c * 65 + 64]
                    V(ctx, "tensor_copy", dstv, pt[gtr % 2][:])
                    tp_ev[gtr] = ctx.cnt["ev"]
            ctx.wait(nc.tensor, "ev")
            last = None
            for c in range(16):
                last = nc.tensor.matmul(pkv[:], ktm[:, c * 64:(c + 1) * 64],
                                        vtm[:, c * 65:(c + 1) * 65],
                                        start=(c == 0), stop=(c == 15))
            ctx.inc(last, "mm")
            ctx.wait(nc.vector, "mm")
            V(ctx, "tensor_copy", kvb[:], pkv[:])
            ctx.wait(nc.tensor, "ev")
            if u >= 2:
                st_grp[u - 2].wait(nc.vector)
            for c in range(16):
                bank = po[c % 2]
                goc = u * 16 + c
                if goc >= 2:
                    ctx.wait(nc.tensor, "ev", o_ev.get(goc - 2, 0))
                ctx.inc(nc.tensor.matmul(bank[:], qsl[p][:, c * 128:(c + 1) * 128],
                                         kvb[:], start=True, stop=True), "mm")
                ctx.wait(nc.vector, "mm")
                V(ctx, "tensor_single_scalar", dd[:], bank[:, 64:65], 1e-6, ALU.add)
                V(ctx, "reciprocal", dd[:], dd[:])
                V(ctx, "tensor_scalar_mul", otm[:], bank[:, 0:64], dd[:])
                o_ev[goc] = ctx.cnt["ev"]
                ctx.wait(nc.tensor, "ev")
                ctx.inc(nc.tensor.transpose(pt2[c % 2][:], otm[:], idn[:]), "mm")
                ctx.wait(nc.vector, "mm")
                V(ctx, "tensor_copy", aob[p][:, c * 128:(c + 1) * 128], pt2[c % 2][:])
            done_mm[u] = ctx.cnt["mm"]
            ctx.wait(nc.gpsimd, "ev")
            gs = DmaGroup(ctx, "st")
            gs.st(T["ao"][h * 64:(h + 1) * 64, b * SEQ:(b + 1) * SEQ], aob[p][:])
            st_grp[u] = gs
            u += 1
    ctx.barrier()
    ctx.pop_frame()



def emit_l1a_proj(ctx, T):
    nc = ctx.nc
    ctx.push_frame()
    nk = D // 128
    nf = D // 128
    wb = load_weight_bf(ctx, "WprojT", T["WprojT"], D, D)
    rhs = [ctx.sbuf(f"pj_r{p}", [128, nk * TT], BF16) for p in range(2)]
    xr = ctx.sbuf("pj_x", [128, 4 * TT], F32)
    NS = 6
    ob = ctx.sbuf("pj_ob", [128, NS * TT], F32)
    ps = [ctx.psum(f"pj_ps{q}", [128, TT]) for q in range(4)]
    done_mm = {}
    slot_st = {}
    x_ev = {}
    gidx = 0
    for t in range(NTT):
        p = t % 2
        if t >= 2:
            ctx.wait(nc.sync, "mm", done_mm[t - 2])
        g = DmaGroup(ctx, "ld")
        for k in range(nk):
            g.ld(rhs[p][:, k * TT:(k + 1) * TT],
                 T["ao"][k * 128:(k + 1) * 128, t * TT:(t + 1) * TT])
        g.wait(nc.tensor)
        if t == 0:
            ctx.wait(nc.tensor, "pp")
        for f in range(nf):
            bank = ps[gidx % 4]
            xslot = gidx % 4
            if gidx >= 4:
                ctx.wait(nc.sync, "ev", x_ev[gidx - 4])
            gx = DmaGroup(ctx, "ldC")
            gx.ld(xr[:, xslot * TT:(xslot + 1) * TT],
                  T["x1"][f * 128:(f + 1) * 128, t * TT:(t + 1) * TT])
            if gidx >= 4:
                ctx.wait(nc.tensor, "ev", x_ev[gidx - 4])
            last = None
            for k in range(nk):
                last = nc.tensor.matmul(bank[:], wb[:, k * D + f * 128: k * D + (f + 1) * 128],
                                        rhs[p][:, k * TT:(k + 1) * TT],
                                        start=(k == 0), stop=(k == nk - 1))
            ctx.inc(last, "mm")
            slot = gidx % NS
            ctx.wait(nc.vector, "mm")
            gx.wait(nc.vector)
            if gidx >= NS:
                slot_st[gidx - NS].wait(nc.vector)
            V(ctx, "scalar_tensor_tensor", ob[:, slot * TT:(slot + 1) * TT], bank[:],
              ctx.R["bproj128"][:, f:f + 1], xr[:, xslot * TT:(xslot + 1) * TT],
              ALU.add, ALU.add)
            x_ev[gidx] = ctx.cnt["ev"]
            ctx.wait(nc.gpsimd, "ev")
            gs = DmaGroup(ctx, "st")
            gs.st(T["p2"][f * 128:(f + 1) * 128, t * TT:(t + 1) * TT],
                  ob[:, slot * TT:(slot + 1) * TT])
            slot_st[gidx] = gs
            gidx += 1
        done_mm[t] = ctx.cnt["mm"]
    ctx.barrier()
    ctx.pop_frame()



def _ucol(s, t0):
    return s * (SEQ + 1) + 1 + t0


def load_shifted_u(ctx, g, ubuf, dst_f32, dst_bf, t):
    nc = ctx.nc
    s, t0 = t // TPS, (t % TPS) * TT
    for k in range(8):
        col = _ucol(s, t0) if k < 4 else _ucol(s, t0) - 1
        g.ld(dst_f32[:, k * TT:(k + 1) * TT],
             ubuf[k * 128:(k + 1) * 128, col:col + TT])
    g.wait(nc.gpsimd)
    for k in range(8):
        P(ctx, "tensor_copy", dst_bf[:, k * TT:(k + 1) * TT],
          dst_f32[:, k * TT:(k + 1) * TT])


def emit_f_qkv(ctx, T, i, ubuf):
    nc = ctx.nc
    ctx.push_frame()
    nm = f"fq{i}_"
    nk = 8
    stg = [ctx.sbuf(nm + "stg0", [128, 2048], F32), ctx.sbuf(nm + "stg1", [128, 2048], F32)]
    wqk = load_weight_bf(ctx, nm + "wqk", T["Wqkv6"][i], D, 2048, colmap=(0, 2048), stg=stg)
    wv = load_weight_bf(ctx, nm + "wv", T["Wqkv6"][i], D, 1024, colmap=(2048, 1024), stg=stg)
    uf = [ctx.sbuf(nm + f"uf{p}", [128, 8 * TT], F32) for p in range(2)]
    ub = [ctx.sbuf(nm + f"ub{p}", [128, 8 * TT], BF16) for p in range(2)]
    sqt = ctx.sbuf(nm + "sqt", [128, TT], BF16)
    qf = ctx.sbuf(nm + "qf", [128, TT], F32)
    sS = ctx.sbuf(nm + "sS", [2, TT], F32)
    qn = ctx.sbuf(nm + "qn", [128, TT], BF16)
    rot = ctx.sbuf(nm + "rot", [128, TT], BF16)
    tt1 = ctx.sbuf(nm + "tt1", [128, TT], BF16)
    NS = 4
    qr = ctx.sbuf(nm + "qr", [128, NS * TT], BF16)
    vo = ctx.sbuf(nm + "vo", [128, NS * TT], BF16)
    ps = [ctx.psum(nm + f"ps{q}", [128, TT]) for q in range(2)]
    psS = ctx.psum(nm + "psS", [2, TT])
    psB = ctx.psum(nm + "psB", [128, TT])
    psV = [ctx.psum(nm + f"psV{q}", [128, TT]) for q in range(2)]
    done_mm = {}
    slot_st = {}
    vslot_st = {}
    qk_ac = {}
    psS_free = {}
    psB_free = {}
    v_ev = {}
    gidx = 0
    vgidx = 0
    for t in range(NTT):
        p = t % 2
        s, t0 = t // TPS, (t % TPS) * TT
        if t >= 2:
            ctx.wait(nc.sync, "mm", done_mm[t - 2])
            ctx.wait(nc.sync, "ev", done_mm[(t - 2, "ev")])
            ctx.wait(nc.sync, "pp", done_mm[(t - 2, "pp")])
        g = DmaGroup(ctx, "ld")
        load_shifted_u(ctx, g, ubuf, uf[p], ub[p], t)
        ctx.wait(nc.tensor, "pp")
        ctx.wait(nc.tensor, "ev")
        if t == 0:
            pass
        for f in range(16):
            isq = f < 8
            fp = f % 8
            bank = ps[gidx % 2]
            cs = ctx.R["csq" if isq else "csk"][:, t0:t0 + TT]
            ss = ctx.R["ssq" if isq else "ssk"][:, t0:t0 + TT]
            sc6 = ctx.R["qs6" if isq else "ks6"][:, i:i + 1]
            if gidx >= 2:
                ctx.wait(nc.tensor, "ac", qk_ac.get(gidx - 2, 0))
            last = None
            for k in range(nk):
                last = nc.tensor.matmul(bank[:], wqk[:, k * 2048 + f * 128:k * 2048 + (f + 1) * 128],
                                        ub[p][:, k * TT:(k + 1) * TT],
                                        start=(k == 0), stop=(k == nk - 1))
            ctx.inc(last, "mm")
            ctx.wait(nc.scalar, "mm")
            ctx.wait(nc.scalar, "ev")
            S(ctx, "activation", sqt[:], bank[:], AF.Square, bias=0.0, scale=1.0)
            S(ctx, "activation", qf[:], bank[:], AF.Copy, bias=0.0, scale=1.0)
            qk_ac[gidx] = ctx.cnt["ac"]
            ctx.wait(nc.tensor, "ac")
            if gidx >= 1:
                ctx.wait(nc.tensor, "ev", psS_free.get(gidx - 1, 0))
            ctx.inc(nc.tensor.matmul(psS[:], ctx.R["ones2"][:], sqt[:],
                                     start=True, stop=True), "mm")
            ctx.wait(nc.vector, "mm")
            V(ctx, "tensor_single_scalar", sS[:], psS[:], 1e-24, ALU.max)
            psS_free[gidx] = ctx.cnt["ev"]
            ctx.wait(nc.scalar, "ev")
            S(ctx, "activation", sS[:], sS[:], AF.Ln, bias=0.0, scale=1.0)
            S(ctx, "activation", sS[:], sS[:], AF.Exp, bias=0.0, scale=-0.5)
            ctx.wait(nc.tensor, "ac")
            if gidx >= 1:
                ctx.wait(nc.tensor, "ev", psB_free.get(gidx - 1, 0))
            ctx.inc(nc.tensor.matmul(psB[:], ctx.R["sel2f"][:], sS[:],
                                     start=True, stop=True), "mm")
            ctx.wait(nc.vector, "mm")
            ctx.wait(nc.vector, "ac")
            V(ctx, "scalar_tensor_tensor", qn[:], qf[:], sc6, psB[:], ALU.mult, ALU.mult)
            psB_free[gidx] = ctx.cnt["ev"]
            V(ctx, "tensor_scalar_mul", rot[0:32, :], qn[32:64, :], -1.0)
            V(ctx, "tensor_copy", rot[32:64, :], qn[0:32, :])
            V(ctx, "tensor_scalar_mul", rot[64:96, :], qn[96:128, :], -1.0)
            V(ctx, "tensor_copy", rot[96:128, :], qn[64:96, :])
            slot = gidx % NS
            if gidx >= NS:
                slot_st[gidx - NS].wait(nc.vector)
            V(ctx, "tensor_tensor", tt1[:], qn[:], cs, ALU.mult)
            V(ctx, "tensor_tensor", qr[:, slot * TT:(slot + 1) * TT], rot[:], ss, ALU.mult)
            V(ctx, "tensor_tensor", qr[:, slot * TT:(slot + 1) * TT], tt1[:],
              qr[:, slot * TT:(slot + 1) * TT], ALU.add)
            ctx.wait(nc.gpsimd, "ev")
            gs = DmaGroup(ctx, "st")
            if isq:
                gs.st(T["qscr"][fp * 128:(fp + 1) * 128, s * SEQ + t0:s * SEQ + t0 + TT],
                      qr[:, slot * TT:(slot + 1) * TT])
            else:
                gs.st(T["kscr"][fp * 128:(fp + 1) * 128,
                                s * (WIN + SEQ) + WIN + t0:s * (WIN + SEQ) + WIN + t0 + TT],
                      qr[:, slot * TT:(slot + 1) * TT])
            slot_st[gidx] = gs
            gidx += 1
        for c4 in range(4):
            for c2 in range(2):
                bank = psV[vgidx % 2]
                if vgidx >= 2:
                    ctx.wait(nc.tensor, "ev", v_ev.get(vgidx - 2, 0))
                last = None
                for k in range(nk):
                    last = nc.tensor.matmul(bank[:], ub[p][:, k * TT + c4 * 128:k * TT + (c4 + 1) * 128],
                                            wv[:, k * 1024 + c2 * TT:k * 1024 + (c2 + 1) * TT],
                                            start=(k == 0), stop=(k == nk - 1))
                ctx.inc(last, "mm")
                vslot = vgidx % NS
                ctx.wait(nc.vector, "mm")
                if vgidx >= NS:
                    vslot_st[vgidx - NS].wait(nc.vector)
                V(ctx, "tensor_copy", vo[:, vslot * TT:(vslot + 1) * TT], bank[:])
                v_ev[vgidx] = ctx.cnt["ev"]
                ctx.wait(nc.gpsimd, "ev")
                gs = DmaGroup(ctx, "stB")
                j = (t % TPS) * 4 + c4
                for hh in range(8):
                    h = c2 * 8 + hh
                    base = (s * 16 + h) * 18 * 128
                    gs.st(T["vscr"][base + (j + 2) * 128:base + (j + 3) * 128, :],
                          vo[:, vslot * TT + hh * 64:vslot * TT + (hh + 1) * 64])
                vslot_st[vgidx] = gs
                vgidx += 1
        done_mm[t] = ctx.cnt["mm"]
        done_mm[(t, "ev")] = ctx.cnt["ev"]
        done_mm[(t, "pp")] = ctx.cnt["pp"]
    ctx.barrier()
    ctx.pop_frame()


def emit_f_attn(ctx, T, i):
    nc = ctx.nc
    ctx.push_frame()
    nm = f"fa{i}_"
    qsl = [ctx.sbuf(nm + f"q{p}", [64, SEQ], BF16) for p in range(2)]
    ksl = [ctx.sbuf(nm + f"k{p}", [64, WIN + SEQ], BF16) for p in range(2)]
    vsl = [ctx.sbuf(nm + f"v{p}", [128, 18 * 64], BF16) for p in range(2)]
    nmx = ctx.sbuf(nm + "nm", [128, 4], F32)
    ea = ctx.sbuf(nm + "ea", [128, 2 * 384], BF16)
    eaT = ctx.sbuf(nm + "eaT", [128, 2 * 384], BF16)
    aob = [ctx.sbuf(nm + f"ao{p}", [64, SEQ], BF16) for p in range(2)]
    psS = [ctx.psum(nm + f"psS{q}", [128, 384], F32) for q in range(2)]
    psT = [ctx.psum(nm + f"psT{q}", [128, 384], BF16) for q in range(2)]
    psO = [ctx.psum(nm + f"psO{q}", [64, 128], F32) for q in range(2)]
    idn = ctx.R["ident"]
    done_mm = {}
    st_grp = {}
    fa_ac = {}
    fa_mx = {}
    u = 0
    for b in range(B_CORE):
        for h in range(16):
            p = u % 2
            if u >= 2:
                ctx.wait(nc.sync, "mm", done_mm[u - 2])
                ctx.wait(nc.sync, "ev", done_mm[(u - 2, "ev")])
            g = DmaGroup(ctx, "ld")
            g.ld(qsl[p][:], T["qscr"][(h // 2) * 128 + (h % 2) * 64:(h // 2) * 128 + (h % 2) * 64 + 64,
                                      b * SEQ:(b + 1) * SEQ])
            g.ld(ksl[p][:], T["kscr"][(h // 2) * 128 + (h % 2) * 64:(h // 2) * 128 + (h % 2) * 64 + 64,
                                      b * (WIN + SEQ):(b + 1) * (WIN + SEQ)])
            base = (b * 16 + h) * 18 * 128
            g.ld(vsl[p][:].rearrange("q (j e) -> q j e", j=18),
                 T["vscr"][base:base + 18 * 128, :].rearrange("(j q) e -> q j e", q=128))
            g.wait(nc.tensor)
            if u >= 2:
                st_grp[u - 2].wait(nc.vector)
            for j in range(16):
                q2 = j % 2
                bS = psS[q2]
                bO = psO[q2]
                gj = u * 16 + j
                eav = ea[:, q2 * 384:(q2 + 1) * 384]
                eTv = eaT[:, q2 * 384:(q2 + 1) * 384]
                variant = 2 if j == 0 else (1 if j == 1 else 0)
                if gj >= 2:
                    ctx.wait(nc.tensor, "ac", fa_ac.get(gj - 2, 0))
                    ctx.wait(nc.tensor, "ev", fa_mx.get(gj - 2, 0))
                nc.tensor.matmul(bS[:], qsl[p][:, j * 128:(j + 1) * 128],
                                 ksl[p][:, j * 128:j * 128 + 384],
                                 start=True, stop=False)
                ctx.inc(nc.tensor.matmul(bS[:], idn[:],
                                         ctx.R["masks"][:, variant * 384:(variant + 1) * 384],
                                         start=False, stop=True), "mm")
                ctx.wait(nc.vector, "mm")
                V(ctx, "tensor_reduce", nmx[:, q2 * 2:q2 * 2 + 1], bS[:], axis=AX.X,
                  op=ALU.max, negate=True)
                fa_mx[gj] = ctx.cnt["ev"]
                ctx.wait(nc.scalar, "ev")
                ctx.wait(nc.scalar, "mm")
                S(ctx, "activation", eav, bS[:], AF.Exp, bias=nmx[:, q2 * 2:q2 * 2 + 1],
                  scale=1.0, accum_out=nmx[:, q2 * 2 + 1:q2 * 2 + 2])
                fa_ac[gj] = ctx.cnt["ac"]
                ctx.wait(nc.vector, "ac")
                V(ctx, "reciprocal", nmx[:, q2 * 2 + 1:q2 * 2 + 2],
                  nmx[:, q2 * 2 + 1:q2 * 2 + 2])
                V(ctx, "tensor_scalar_mul", eav, eav, nmx[:, q2 * 2 + 1:q2 * 2 + 2])
                ctx.wait(nc.tensor, "ev")
                last = None
                for c in range(3):
                    last = nc.tensor.transpose(psT[q2][:, c * 128:(c + 1) * 128],
                                               eav[:, c * 128:(c + 1) * 128], idn[:])
                ctx.inc(last, "mm")
                ctx.wait(nc.vector, "mm")
                V(ctx, "tensor_copy", eTv, psT[q2][:])
                ctx.wait(nc.tensor, "ev")
                last = None
                for c in range(3):
                    last = nc.tensor.matmul(bO[:], vsl[p][:, (j + c) * 64:(j + c + 1) * 64],
                                            eTv[:, c * 128:(c + 1) * 128],
                                            start=(c == 0), stop=(c == 2))
                ctx.inc(last, "mm")
                ctx.wait(nc.vector, "mm")
                V(ctx, "tensor_copy", aob[p][:, j * 128:(j + 1) * 128], bO[:])
            done_mm[u] = ctx.cnt["mm"]
            done_mm[(u, "ev")] = ctx.cnt["ev"]
            ctx.wait(nc.gpsimd, "ev")
            gs = DmaGroup(ctx, "st")
            gs.st(T["ao"][h * 64:(h + 1) * 64, b * SEQ:(b + 1) * SEQ], aob[p][:])
            st_grp[u] = gs
            u += 1
    ctx.barrier()
    ctx.pop_frame()


def emit_f_wo(ctx, T, i, ubuf_in, ubuf_out):
    nc = ctx.nc
    ctx.push_frame()
    nm = f"fw{i}_"
    nk = 8
    nf = 8
    wb = load_weight_bf(ctx, nm + "wo", T["Wo6"][i], D, D)
    rhs = [ctx.sbuf(nm + f"r{p}", [128, nk * TT], BF16) for p in range(2)]
    xr = ctx.sbuf(nm + "x", [128, 4 * TT], F32)
    NS = 6
    ob = ctx.sbuf(nm + "ob", [128, NS * TT], F32)
    ps = [ctx.psum(nm + f"ps{q}", [128, TT]) for q in range(4)]
    done_mm = {}
    slot_st = {}
    x_ev = {}
    gidx = 0
    for t in range(NTT):
        p = t % 2
        s, t0 = t // TPS, (t % TPS) * TT
        if t >= 2:
            ctx.wait(nc.sync, "mm", done_mm[t - 2])
        g = DmaGroup(ctx, "ld")
        for k in range(nk):
            g.ld(rhs[p][:, k * TT:(k + 1) * TT],
                 T["ao"][k * 128:(k + 1) * 128, t * TT:(t + 1) * TT])
        g.wait(nc.tensor)
        if t == 0:
            ctx.wait(nc.tensor, "pp")
        for f in range(nf):
            bank = ps[gidx % 4]
            xslot = gidx % 4
            if gidx >= 4:
                ctx.wait(nc.sync, "ev", x_ev[gidx - 4])
            gx = DmaGroup(ctx, "ldC")
            gx.ld(xr[:, xslot * TT:(xslot + 1) * TT],
                  ubuf_in[f * 128:(f + 1) * 128, _ucol(s, t0):_ucol(s, t0) + TT])
            if gidx >= 4:
                ctx.wait(nc.tensor, "ev", x_ev[gidx - 4])
            last = None
            for k in range(nk):
                last = nc.tensor.matmul(bank[:], wb[:, k * D + f * 128: k * D + (f + 1) * 128],
                                        rhs[p][:, k * TT:(k + 1) * TT],
                                        start=(k == 0), stop=(k == nk - 1))
            ctx.inc(last, "mm")
            slot = gidx % NS
            ctx.wait(nc.vector, "mm")
            gx.wait(nc.vector)
            if gidx >= NS:
                slot_st[gidx - NS].wait(nc.vector)
            V(ctx, "tensor_tensor", ob[:, slot * TT:(slot + 1) * TT], bank[:],
              xr[:, xslot * TT:(xslot + 1) * TT], ALU.add)
            x_ev[gidx] = ctx.cnt["ev"]
            ctx.wait(nc.gpsimd, "ev")
            gs = DmaGroup(ctx, "st")
            gs.st(ubuf_out[f * 128:(f + 1) * 128, _ucol(s, t0):_ucol(s, t0) + TT],
                  ob[:, slot * TT:(slot + 1) * TT])
            slot_st[gidx] = gs
            gidx += 1
        done_mm[t] = ctx.cnt["mm"]
    ctx.barrier()
    ctx.pop_frame()


def emit_f_ln(ctx, T, i, ubuf):
    nc = ctx.nc
    ctx.push_frame()
    nm = f"fl{i}_"
    nk = 8
    xb = [ctx.sbuf(nm + f"x{p}", [128, nk * TT], F32) for p in range(2)]
    sq = [ctx.sbuf(nm + f"q{p}", [128, nk * TT], BF16) for p in range(2)]
    ob = [ctx.sbuf(nm + f"ob{p}", [128, nk * TT], BF16) for p in range(2)]
    sc = ctx.sbuf(nm + "sc", [1, 8 * TT], F32)
    rb = [ctx.sbuf(nm + f"rb{p}", [128, TT], F32) for p in range(2)]
    ot = [ctx.sbuf(nm + f"o{p}", [128, TT], F32) for p in range(2)]
    psA = [ctx.psum(nm + "psA", [1, TT])] * 2
    psB = [ctx.psum(nm + "psB", [1, TT])] * 2
    psM = [ctx.psum(nm + "psM", [128, TT])] * 2
    psR = [ctx.psum(nm + "psR", [128, TT])] * 2
    done_ev = {}
    done_sq = {}
    st_grp = {}
    for t in range(NTT):
        p = t % 2
        if t >= 2:
            ctx.wait(nc.sync, "ev", done_ev[t - 2])
            ctx.wait(nc.sync, "ac", done_sq[t - 2])
        g = DmaGroup(ctx, "ld")
        s, t0 = t // TPS, (t % TPS) * TT
        for k in range(nk):
            col = _ucol(s, t0) if k < 4 else _ucol(s, t0) - 1
            g.ld(xb[p][:, k * TT:(k + 1) * TT],
                 ubuf[k * 128:(k + 1) * 128, col:col + TT])
        g.wait(nc.scalar)
        for k in range(nk):
            S(ctx, "activation", sq[p][:, k * TT:(k + 1) * TT],
              xb[p][:, k * TT:(k + 1) * TT], AF.Square, bias=0.0, scale=1.0)
        done_sq[t] = ctx.cnt["ac"]
        ctx.wait(nc.tensor, "ev")
        ctx.wait(nc.tensor, "ac")
        for k in range(nk):
            nc.tensor.matmul(psA[p][:], ctx.R["ones128f"][:], xb[p][:, k * TT:(k + 1) * TT],
                             start=(k == 0), stop=(k == nk - 1))
        last = None
        for k in range(nk):
            last = nc.tensor.matmul(psB[p][:], ctx.R["ones128b"][:], sq[p][:, k * TT:(k + 1) * TT],
                                    start=(k == 0), stop=(k == nk - 1))
        ctx.inc(last, "mm")
        mu = sc[:, (4 * p + 0) * TT:(4 * p + 1) * TT]
        vv = sc[:, (4 * p + 1) * TT:(4 * p + 2) * TT]
        ms = sc[:, (4 * p + 2) * TT:(4 * p + 3) * TT]
        rs = sc[:, (4 * p + 3) * TT:(4 * p + 4) * TT]
        ctx.wait(nc.scalar, "mm")
        S(ctx, "activation", mu, psA[p][:], AF.Copy, bias=0.0, scale=1.0 / D)
        S(ctx, "activation", vv, psB[p][:], AF.Copy, bias=0.0, scale=1.0 / D)
        S(ctx, "activation", ms, mu, AF.Square, bias=0.0, scale=1.0)
        ctx.wait(nc.vector, "ac")
        V(ctx, "tensor_tensor", vv, vv, ms, ALU.subtract)
        ctx.wait(nc.scalar, "ev")
        S(ctx, "activation", vv, vv, AF.Ln, bias=cbias(ctx, 1e-5, 1), scale=1.0)
        S(ctx, "activation", rs, vv, AF.Exp, bias=0.0, scale=-0.5)
        ctx.wait(nc.tensor, "ac")
        bcast_cols(ctx, psM[p][:], mu)
        ctx.inc(bcast_cols(ctx, psR[p][:], rs), "mm")
        ctx.wait(nc.vector, "mm")
        if t >= 2:
            st_grp[t - 2].wait(nc.vector)
        V(ctx, "tensor_copy", rb[p][:], psR[p][:])
        for k in range(nk):
            V(ctx, "tensor_tensor", ot[p][:], xb[p][:, k * TT:(k + 1) * TT],
              psM[p][:], ALU.subtract)
            V(ctx, "tensor_tensor", ot[p][:], ot[p][:], rb[p][:], ALU.mult)
            V(ctx, "tensor_scalar", ob[p][:, k * TT:(k + 1) * TT], ot[p][:],
              ctx.R["flng128"][:, i * 8 + k:i * 8 + k + 1],
              ctx.R["flnb128"][:, i * 8 + k:i * 8 + k + 1], ALU.mult, ALU.add)
        done_ev[t] = ctx.cnt["ev"]
        ctx.wait(nc.gpsimd, "ev")
        gs = DmaGroup(ctx, "st")
        for k in range(nk):
            gs.st(T["xnf"][k * 128:(k + 1) * 128, t * TT:(t + 1) * TT],
                  ob[p][:, k * TT:(k + 1) * TT])
        st_grp[t] = gs
    ctx.barrier()
    ctx.pop_frame()


def emit_f_w1(ctx, T, i):
    nc = ctx.nc
    ctx.push_frame()
    nm = f"fm{i}_"
    nk = 8
    nfc = FFN_IP // 128
    wb = load_weight_bf(ctx, nm + "w1", T["W1p"][i], D, 2 * FFN_IP)
    rhs = [ctx.sbuf(nm + f"r{p}", [128, nk * TT], BF16) for p in range(2)]
    gel = ctx.sbuf(nm + "gel", [128, TT], F32)
    NS = 4
    ob = ctx.sbuf(nm + "ob", [128, NS * TT], BF16)
    ps = [ctx.psum(nm + f"ps{q}", [128, TT]) for q in range(4)]
    done_mm = {}
    slot_st = {}
    w1_ac = {}
    w1_ev = {}
    gidx = 0
    W = 2 * FFN_IP
    for t in range(NTT):
        p = t % 2
        if t >= 2:
            ctx.wait(nc.sync, "mm", done_mm[t - 2])
        g = DmaGroup(ctx, "ld")
        for k in range(nk):
            g.ld(rhs[p][:, k * TT:(k + 1) * TT],
                 T["xnf"][k * 128:(k + 1) * 128, t * TT:(t + 1) * TT])
        g.wait(nc.tensor)
        if t == 0:
            ctx.wait(nc.tensor, "pp")
        for fc in range(nfc):
            bg = ps[(gidx % 2) * 2]
            ba = ps[(gidx % 2) * 2 + 1]
            if gidx >= 2:
                ctx.wait(nc.tensor, "ac", w1_ac.get(gidx - 2, 0))
                ctx.wait(nc.tensor, "ev", w1_ev.get(gidx - 2, 0))
            for k in range(nk):
                nc.tensor.matmul(bg[:], wb[:, k * W + (FFN_IP + fc * 128):k * W + (FFN_IP + (fc + 1) * 128)],
                                 rhs[p][:, k * TT:(k + 1) * TT],
                                 start=(k == 0), stop=(k == nk - 1))
            last = None
            for k in range(nk):
                last = nc.tensor.matmul(ba[:], wb[:, k * W + fc * 128:k * W + (fc + 1) * 128],
                                        rhs[p][:, k * TT:(k + 1) * TT],
                                        start=(k == 0), stop=(k == nk - 1))
            ctx.inc(last, "mm")
            ctx.wait(nc.scalar, "mm")
            ctx.wait(nc.scalar, "ev")
            S(ctx, "activation", gel[:], bg[:], AF.Gelu,
              bias=ctx.R["b1p128"][:, i * 44 + 22 + fc:i * 44 + 22 + fc + 1], scale=1.0)
            slot = gidx % NS
            ctx.wait(nc.vector, "ac")
            ctx.wait(nc.vector, "mm")
            if gidx >= NS:
                slot_st[gidx - NS].wait(nc.vector)
            V(ctx, "scalar_tensor_tensor", ob[:, slot * TT:(slot + 1) * TT], ba[:],
              ctx.R["b1p128"][:, i * 44 + fc:i * 44 + fc + 1], gel[:],
              ALU.add, ALU.mult)
            w1_ac[gidx] = ctx.cnt["ac"]
            w1_ev[gidx] = ctx.cnt["ev"]
            ctx.wait(nc.gpsimd, "ev")
            gs = DmaGroup(ctx, "st")
            gs.st(T["hgt"][fc * 128:(fc + 1) * 128, t * TT:(t + 1) * TT],
                  ob[:, slot * TT:(slot + 1) * TT])
            slot_st[gidx] = gs
            gidx += 1
        done_mm[t] = ctx.cnt["mm"]
    ctx.barrier()
    ctx.pop_frame()


def emit_f_w2(ctx, T, i, ubuf_in, ubuf_out):
    nc = ctx.nc
    ctx.push_frame()
    nm = f"fz{i}_"
    nk = FFN_IP // 128
    nf = 8
    wb = load_weight_bf(ctx, nm + "w2", T["W2p"][i], FFN_IP, D)
    rhs = [ctx.sbuf(nm + f"r{p}", [128, nk * TT], BF16) for p in range(2)]
    xr = ctx.sbuf(nm + "x", [128, 4 * TT], F32)
    NS = 6
    ob = ctx.sbuf(nm + "ob", [128, NS * TT], F32)
    ps = [ctx.psum(nm + f"ps{q}", [128, TT]) for q in range(4)]
    done_mm = {}
    slot_st = {}
    x_ev = {}
    gidx = 0
    for t in range(NTT):
        p = t % 2
        s, t0 = t // TPS, (t % TPS) * TT
        if t >= 2:
            ctx.wait(nc.sync, "mm", done_mm[t - 2])
        g = DmaGroup(ctx, "ld")
        for k in range(nk):
            g.ld(rhs[p][:, k * TT:(k + 1) * TT],
                 T["hgt"][k * 128:(k + 1) * 128, t * TT:(t + 1) * TT])
        g.wait(nc.tensor)
        if t == 0:
            ctx.wait(nc.tensor, "pp")
        for f in range(nf):
            bank = ps[gidx % 4]
            xslot = gidx % 4
            if gidx >= 4:
                ctx.wait(nc.sync, "ev", x_ev[gidx - 4])
            gx = DmaGroup(ctx, "ldC")
            gx.ld(xr[:, xslot * TT:(xslot + 1) * TT],
                  ubuf_in[f * 128:(f + 1) * 128, _ucol(s, t0):_ucol(s, t0) + TT])
            if gidx >= 4:
                ctx.wait(nc.tensor, "ev", x_ev[gidx - 4])
            last = None
            for k in range(nk):
                last = nc.tensor.matmul(bank[:], wb[:, k * D + f * 128: k * D + (f + 1) * 128],
                                        rhs[p][:, k * TT:(k + 1) * TT],
                                        start=(k == 0), stop=(k == nk - 1))
            ctx.inc(last, "mm")
            slot = gidx % NS
            ctx.wait(nc.vector, "mm")
            gx.wait(nc.vector)
            if gidx >= NS:
                slot_st[gidx - NS].wait(nc.vector)
            V(ctx, "scalar_tensor_tensor", ob[:, slot * TT:(slot + 1) * TT], bank[:],
              ctx.R["b2128"][:, i * 8 + f:i * 8 + f + 1], xr[:, xslot * TT:(xslot + 1) * TT],
              ALU.add, ALU.add)
            x_ev[gidx] = ctx.cnt["ev"]
            ctx.wait(nc.gpsimd, "ev")
            gs = DmaGroup(ctx, "st")
            gs.st(ubuf_out[f * 128:(f + 1) * 128, _ucol(s, t0):_ucol(s, t0) + TT],
                  ob[:, slot * TT:(slot + 1) * TT])
            slot_st[gidx] = gs
            gidx += 1
        done_mm[t] = ctx.cnt["mm"]
    ctx.barrier()
    ctx.pop_frame()



def emit_final(ctx, T, ubuf):
    nc = ctx.nc
    ctx.push_frame()
    nf = 8
    a = [ctx.sbuf(f"fn_a{p}", [128, nf * TT], F32) for p in range(2)]
    b = [ctx.sbuf(f"fn_b{p}", [128, nf * TT], F32) for p in range(2)]
    done_ev = {}
    st_grp = {}
    for t in range(NTT):
        p = t % 2
        s, t0 = t // TPS, (t % TPS) * TT
        if t >= 2:
            ctx.wait(nc.sync, "pp", done_ev[t - 2])
            st_grp[t - 2].wait(nc.sync)
        g = DmaGroup(ctx, "ld")
        for f in range(nf):
            g.ld(a[p][:, f * TT:(f + 1) * TT],
                 T["p2"][f * 128:(f + 1) * 128, t * TT:(t + 1) * TT])
            g.ld(b[p][:, f * TT:(f + 1) * TT],
                 ubuf[f * 128:(f + 1) * 128, _ucol(s, t0):_ucol(s, t0) + TT])
        g.wait(nc.gpsimd)
        for f in range(nf):
            P(ctx, "tensor_tensor", a[p][:, f * TT:(f + 1) * TT],
              a[p][:, f * TT:(f + 1) * TT], b[p][:, f * TT:(f + 1) * TT], ALU.add)
        done_ev[t] = ctx.cnt["pp"]
        gs = DmaGroup(ctx, "st")
        for f in range(nf):
            gs.st(T["out_fm"][f * 128:(f + 1) * 128, t * TT:(t + 1) * TT],
                  a[p][:, f * TT:(f + 1) * TT])
        st_grp[t] = gs
    ctx.barrier()
    ctx.pop_frame()



PHASES = ["p0", "p1", "p2", "p3", "p4", "p5", "p6", "p7", "p8", "p9",
          "f0", "f1", "f2", "f3", "f4", "f5", "p11"]

def build_program(upto=None, dbg=()):
    nc = bass.Bass("TRN2", target_bir_lowering=False)
    T = declare_tensors(nc, dbg)
    ctx = Ctx(nc)
    def stop(name):
        return upto is not None and PHASES.index(upto) < PHASES.index(name)
    emit_consts(ctx, T)
    if not stop("p0"):
        emit_outer_ln(ctx, T, T["x_fm"], dst_bf=T["xn0b"], dst_f32=T["xn0f"], pfx="ln0")
    if not stop("p1"):
        emit_l0_inproj(ctx, T)
    if not stop("p2"):
        emit_l0_conv(ctx, T)
    if not stop("p3"):
        emit_l0_dt(ctx, T)
    if not stop("p4"):
        emit_l0_scan(ctx, T)
    if not stop("p5"):
        emit_l0_outproj(ctx, T)
    if not stop("p6"):
        emit_moe_gate(ctx, T)
    if not stop("p7"):
        emit_moe_expert(ctx, T, -1)
        for e in range(8):
            emit_moe_expert(ctx, T, e)
    if not stop("p8"):
        emit_outer_ln(ctx, T, T["x1"], dst_bf=T["xn1b"], dst_f32=T["uA"],
                      dst_zerocol=True, pfx="ln1")
    if not stop("p9"):
        emit_l1a_qkv(ctx, T)
        emit_l1a_core(ctx, T)
        emit_l1a_proj(ctx, T)
    bufs = [T["uA"], T["uB"]]
    for i in range(6):
        if stop(f"f{i}"):
            break
        emit_f_qkv(ctx, T, i, bufs[0])
        emit_f_attn(ctx, T, i)
        emit_f_wo(ctx, T, i, bufs[0], bufs[1])
        emit_f_ln(ctx, T, i, bufs[1])
        emit_f_w1(ctx, T, i)
        emit_f_w2(ctx, T, i, bufs[1], bufs[0])
    if not stop("p11"):
        emit_final(ctx, T, bufs[0])
    ctx.close()
    return nc

def kernel(**inputs):
    sh, percore = prep_inputs(inputs)
    nc = build_program()
    in_maps = [dict(sh, **pc) for pc in percore]
    res = run_bass_kernel_spmd(nc, in_maps, core_ids=list(range(NCORES)))
    out = np.empty((32, 2048, 1024), np.float32)
    for c in range(NCORES):
        out[c * B_CORE:(c + 1) * B_CORE] = \
            res.results[c]["out_fm"].T.reshape(B_CORE, SEQ, D)
    return out
